# version 15 (final)
# baseline (speedup 1.0000x reference)
import numpy as np
from contextlib import ExitStack
import concourse.bass as bass
import concourse.mybir as mybir
from concourse.bass_utils import run_bass_kernel_spmd

F32 = mybir.dt.float32
BF16 = mybir.dt.bfloat16
I32 = mybir.dt.int32
ALU = mybir.AluOpType
AF = mybir.ActivationFunctionType
AX = mybir.AxisListType

B, T, D, NH, DM, DH = 2, 2048, 1024, 16, 64, 4096
NC8, NCH = 8, 32
TBA, TBD = 512, 256
NTA, NTD, NTT = T // TBA, T // TBD, T // 128
EPS = 1e-5
KB = 1024


class Ctr:
    def __init__(self, h):
        self.h = h
        self.c = 0

    def inc(self, n=1):
        self.c += n
        return self.c


def build_program(dbg=(), race_debug=False, self_sync=True, sync_engines=('v',), sync_lag=1, sync_mode='chain'):
    nc = bass.Bass("TRN2", target_bir_lowering=False,
                   detect_race_conditions=race_debug)

    x_d = nc.dram_tensor("x", [B, D, T], F32, kind="ExternalInput").ap()
    wk_d = nc.dram_tensor("wk", [D, D], F32, kind="ExternalInput").ap()
    wq_d = nc.dram_tensor("wq", [D, D], F32, kind="ExternalInput").ap()
    wv_d = nc.dram_tensor("wv", [D, D], F32, kind="ExternalInput").ap()
    wb_d = nc.dram_tensor("wb", [D, NH], F32, kind="ExternalInput").ap()
    wo_d = nc.dram_tensor("wo", [D, D], F32, kind="ExternalInput").ap()
    w1t_d = nc.dram_tensor("w1t", [D, DH], F32, kind="ExternalInput").ap()
    w2t_d = nc.dram_tensor("w2t", [DH, D], F32, kind="ExternalInput").ap()
    cw_d = nc.dram_tensor("cw", [128, 3 * NCH], F32, kind="ExternalInput").ap()
    cb_d = nc.dram_tensor("cb", [128, NCH], F32, kind="ExternalInput").ap()
    c_d = nc.dram_tensor("cvec", [B, D], F32, kind="ExternalInput").ap()
    wmod_d = nc.dram_tensor("wmod", [D, 6 * D], F32, kind="ExternalInput").ap()
    out_d = nc.dram_tensor("out", [B, D, T], F32, kind="ExternalOutput").ap()
    w1q_d = nc.dram_tensor("w1q_s", [D, DH], BF16, kind="Internal").ap()
    w2q_d = nc.dram_tensor("w2q_s", [DH, D], BF16, kind="Internal").ap()
    x1_d = nc.dram_tensor("x1_s", [B, D, T], F32, kind="Internal").ap()
    h2q_d = nc.dram_tensor("h2q_s", [B, D, T], BF16, kind="Internal").ap()
    sc1_d = nc.dram_tensor("sc1_s", [B, T], F32, kind="Internal").ap()

    dbg_d = {}
    def ddbg(name, shape, dt):
        if name in dbg:
            dbg_d[name] = nc.dram_tensor(name, shape, dt, kind="ExternalOutput").ap()
    ddbg("d_mod", [128, B * 6 * NC8], F32)
    ddbg("d_h", [B, NTA, 128, NC8, TBA], BF16)
    ddbg("d_k", [128, NC8, T], BF16)
    ddbg("d_q", [128, NC8, T], BF16)
    ddbg("d_v", [128, NTT, D], BF16)
    ddbg("d_beta", [128, NTT, NH], F32)
    ddbg("d_o", [B, 128, NC8, T], BF16)
    ddbg("d_x1", [B, D, T], F32)
    ddbg("d_h2q", [B, D, T], BF16)
    ddbg("d_h1", [B, NTD, 128, NCH, TBD + 2], BF16)
    ddbg("d_hs", [B, NTD, 128, NCH, TBD], BF16)
    ddbg("d_wq1", [D, DH], BF16)

    E = {"t": [], "v": [], "a": [], "g": [], "s": []}

    def em(eng, fn):
        E[eng].append(fn)

    def wait(eng, ctr, val):
        if val > 0:
            em(eng, lambda e, h=ctr.h, v=val: e.wait_ge(h, v))

    stack = ExitStack()

    def sem(n):
        return Ctr(stack.enter_context(nc.semaphore(n)))

    sP = sem("sP"); sV = sem("sV"); sA = sem("sA"); sG = sem("sG")
    dX = sem("dX"); dW = sem("dW"); dS = sem("dS")
    dWa = sem("dWa"); dWb = sem("dWb"); dSw = sem("dSw")
    selfsem = {e: sem("slf_" + e) for e in sync_engines} if (self_sync or race_debug) else {}
    lastmark = {}

    def guard(eng):
        for ctr in (sP, sV, sA, sG, dX, dW, dS):
            wait(eng, ctr, ctr.c)

    def guard_all():
        for eng in ("t", "v", "a", "g", "s"):
            guard(eng)

    def sb(name, shape, dt):
        return stack.enter_context(nc.sbuf_tensor(name, shape, dt))

    ones_col = sb("ones_col", [128, 1], F32)
    ident = sb("ident", [128, 128], F32)
    pair_ones = sb("pair_ones", [128, 2], BF16)
    pairsel = sb("pairsel", [2, 128], F32)
    stair = sb("stair", [128, 4, TBA], BF16)
    modv = sb("modv", [128, B * 6 * NC8], F32)
    cw_sb = sb("cw_sb", [128, 3 * NCH], F32)
    cb_sb = sb("cb_sb", [128, NCH], F32)
    beta_sb = sb("beta_sb", [128, NTT, NH], F32)
    ones_row = sb("ones_row", [1, 128], F32)
    mrow = sb("mrow", [1, TBA], F32)
    msrow = sb("msrow", [1, TBA], F32)
    invrow = sb("invrow", [1, TBA], F32)
    mirow = msrow
    ninv = sb("ninv", [2, 2, TBA], F32)
    sc_f1 = sb("sc_f1", [128, TBA], F32)
    sc_f2 = sb("sc_f2", [128, TBA], F32)
    sc_f3 = sb("sc_f3", [128, 2, TBD], F32)
    sc_f4 = sb("sc_f4", [128, TBD], F32)
    convb = sb("convb", [128, 2, TBD], F32)
    s_yrep = sb("s_yrep", [128, TBA], F32)
    mxt = s_yrep
    s_orep = sb("s_orep", [128, TBA], F32)
    sc_b1 = sb("sc_b1", [128, 2, TBD], BF16)
    sc_b2 = sb("sc_b2", [128, 2, TBD], BF16)
    dscr = sb("dscr", [128, 128], BF16)
    swc = sb("swc", [128, 4], F32)
    swn = sb("swn", [128, 2], F32)
    wsum = sb("wsum", [128, 64], F32)
    scal11 = sb("scal11", [1, 8], F32)
    scrow = sb("scrow", [1, TBD + 2], F32)
    epscol = sb("epscol", [128, 1], F32)
    vnp = sb("vnp", [1, 4], F32)
    m4 = sb("m4", [128, 4], F32)
    ksq = sc_b1[:, :, :].rearrange("p a b -> p (a b)")
    scf3f = sc_f3[:, :, :].rearrange("p a b -> p (a b)")
    convbf = convb[:, :, :].rearrange("p a b -> p (a b)")
    scb1f = sc_b1[:, :, :].rearrange("p a b -> p (a b)")
    scb2f = sc_b2[:, :, :].rearrange("p a b -> p (a b)")

    BIG = int(nc.sbuf_bytes_remaining) - 64
    assert BIG >= int(172.25 * KB), f"arena too small: {BIG}"
    arena = nc.alloc_sbuf_tensor("big_arena", [128, BIG], mybir.dt.uint8)
    abase = nc.lookup_mloc(arena).addr

    def at_(name, off_kb, shape, dt):
        return nc.alloc_sbuf_tensor_at(name, shape, dt, offset=abase + int(off_kb * KB))

    K_sb = at_("K_sb", 0, [128, NC8, T], BF16)
    Q_sb = at_("Q_sb", 32, [128, NC8, T], BF16)
    V_sb = at_("V_sb", 64, [128, NTT, D], BF16)
    h_sb = at_("h_sb", 96, [128, NC8, T], BF16)
    O_sb = at_("O_sb", 96, [128, NC8, T], BF16)
    xt_sb = at_("xt_sb", 128, [128, NC8, TBA], F32)
    wk_sb = at_("wk_sb", 128, [128, NC8, D], BF16)
    wbeta_sb = at_("wbeta_sb", 144, [128, NC8, NH], BF16)
    at_sb = at_("at_sb", 128, [128, 2, 2, TBA], BF16)
    wo_sb = at_("wo_sb", 132.25, [128, NC8, D], BF16)
    x1t_sb = at_("x1t_sb", 0, [128, NC8, TBA], F32)
    h2t_sb = at_("h2t_sb", 16, [128, NC8, TBA], F32)
    xtc_sb = at_("xtc_sb", 40, [128, NC8, TBA], F32)
    habsC = at_("habsC", 56, [128, 2, TBA], F32)
    qzC = at_("qzC", 60, [128, 2, TBA], BF16)
    qiC = at_("qiC", 62, [128, TBA], I32)
    modrows = at_("modrows", 48, [2, 6 * D], F32)
    siluc = at_("siluc", 72, [2, D], F32)
    siluct = at_("siluct", 76, [128, NC8 * 2], F32)
    onesr_bf = at_("onesr_bf", 77, [128, TBA], BF16)
    wmod_sb = at_("wmod_sb", 78, [128, 2, NC8, TBA], F32)
    w1q_sb = at_("w1q_sb", 0, [128, NC8, DH], BF16)
    h2q_res = at_("h2q_res", 64, [128, NC8, T], BF16)
    w2s_sb = at_("w2s_sb", 96, [128, 2, NCH, 128], BF16)
    h1s_sb = at_("h1s_sb", 112, [128, NCH, TBD + 2], BF16)
    hs_sb = at_("hs_sb", 128.25, [128, NCH, TBD], BF16)
    q2o_sb = at_("q2o_sb", 144.25, [128, NCH, TBD], BF16)
    outd_sb = at_("outd_sb", 160.25, [128, NC8, TBD], F32)
    qzD = at_("qzD", 168.25, [128, 2, TBA], BF16)
    qiD = at_("qiD", 170.25, [128, TBA], I32)
    wqg_s = at_("wqg_s", 148.25, [128, 4, 1024], F32)
    wqg_q = at_("wqg_q", 164.25, [128, 2, 1024], BF16)
    wqg_c = at_("wqg_c", 168.25, [128, 2, TBA], F32)
    x1t2_sb = at_("x1t2_sb", 148.25, [128, NC8, TBA], F32)
    at3_sb = at_("at3_sb", 148.25, [128, 3, 2, TBA], BF16)
    rowsQ = at_("rowsQ", 32, [1, 3, TBA], F32)

    pb = [stack.enter_context(nc.psum_tensor(f"pb{i}", [128, 512], F32)) for i in range(8)]

    def dma(eng, out, in_, sem_, **kw):
        if sem_ in (dX, dW, dS) and sem_.c > 0:
            em(eng, lambda e, h=sem_.h, v=sem_.c: e.wait_ge(h, v))
        v = sem_.inc(16)
        em(eng, lambda e, o=out, i=in_, h=sem_.h, k=kw:
           e.dma_start(out=o, in_=i, **k).then_inc(h, 16))
        return v

    def mm(out, lh, rh, start, stop, inc=None, tp=None):
        v = inc.inc() if inc is not None else None
        ih = inc.h if inc is not None else None

        def f(e, o=out, l=lh, r=rh, st=start, sp=stop, iv=v, i=ih, tpos=tp):
            ins = e.matmul(o, l, r, start=st, stop=sp, tile_position=tpos)
            if iv is not None:
                ins.then_inc(i, 1)
        em("t", f)
        return v

    def op(eng, meth, *args, inc=None, **kw):
        ss = selfsem.get(eng)
        if ss is not None and sync_mode == "chain":
            if eng in lastmark:
                mctr, mval = lastmark[eng]
                mval -= (sync_lag - 1)
                if mval > 0:
                    wait(eng, mctr, mval)
            mark = inc if inc is not None else ss
            v = inc.inc() if inc is not None else None
            if inc is None:
                ss.inc()
            lastmark[eng] = (mark, mark.c)
            ih = inc.h if inc is not None else None
            use_self = inc is None

            def f(e, m=meth, a=args, k=kw, iv=v, i=ih, sh=(ss.h if use_self else None)):
                ins = getattr(e, m)(*a, **k)
                if iv is not None:
                    ins.then_inc(i, 1)
                if sh is not None:
                    ins.then_inc(sh, 1)
            em(eng, f)
            return v
        if ss is not None and sync_mode == "flush" and inc is not None:
            sv = ss.inc()
            v = inc.inc()

            def f1(e, m=meth, a=args, k=kw, sh=ss.h):
                getattr(e, m)(*a, **k).then_inc(sh, 1)
            em(eng, f1)
            wait(eng, ss, sv)
            em(eng, lambda e, i=inc.h: e.memset(vnp[0:1, 1:2], 0.0).then_inc(i, 1))
            return v
        v = inc.inc() if inc is not None else None
        ih = inc.h if inc is not None else None

        def f(e, m=meth, a=args, k=kw, iv=v, i=ih):
            ins = getattr(e, m)(*a, **k)
            if iv is not None:
                ins.then_inc(i, 1)
        em(eng, f)
        return v

    def xv(b):
        return x_d[b].rearrange("(c p) t -> p c t", p=128)

    def x1v(b):
        return x1_d[b].rearrange("(c p) t -> p c t", p=128)

    def h2qv(b):
        return h2q_d[b].rearrange("(c p) t -> p c t", p=128)

    def outv(b):
        return out_d[b].rearrange("(c p) t -> p c t", p=128)

    def vcol(b, v6, cc):
        i = (b * 6 + v6) * NC8 + cc
        return modv[:, i:i + 1]

    def vcols(b, v6):
        i = (b * 6 + v6) * NC8
        return modv[:, i:i + NC8]

    op("g", "memset", ones_col[:], 1.0, inc=sG)
    op("g", "memset", ones_row[:], 1.0, inc=sG)
    op("g", "memset", epscol[:], EPS, inc=sG)
    op("g", "memset", onesr_bf[:], 1.0, inc=sG)
    op("g", "memset", sc_f1[:], 1.0, inc=sG)
    op("g", "memset", sc_f2[:, 0:2], 1.0, inc=sG)
    wait("g", sG, sG.c)
    op("g", "affine_select", ident[:], sc_f1[:, 0:128], [[-1, 128]], ALU.is_equal, 0.0,
       base=0, channel_multiplier=1, inc=sG)
    op("g", "affine_select", pair_ones[:, :], sc_f2[:, 0:2], [[64, 2]], ALU.is_ge, 0.0,
       base=63, channel_multiplier=-1, inc=sG)
    wait("g", sG, sG.c)
    op("g", "affine_select", pair_ones[:, :], pair_ones[:, :], [[-64, 2]], ALU.is_ge, 0.0,
       base=0, channel_multiplier=1, inc=sG)
    op("g", "affine_select", pairsel[:, :], sc_f1[0:2, 0:128], [[1, 128]], ALU.is_ge, 0.0,
       base=0, channel_multiplier=-64, inc=sG)
    wait("g", sG, sG.c)
    op("g", "affine_select", pairsel[:, :], pairsel[:, :], [[-1, 128]], ALU.is_ge, 0.0,
       base=63, channel_multiplier=64, inc=sG)
    for k in range(4):
        op("g", "affine_select", stair[:, k, :], onesr_bf[:, 0:TBA], [[-1, TBA]], ALU.is_ge,
           0.0, base=128 * k, channel_multiplier=1, inc=sG)
    v_gconst = sG.c

    dma("s", cw_sb[:], cw_d, dX)
    dma("s", cb_sb[:], cb_d, dX)
    dma("s", modrows[0:2, 0:D], c_d, dX)
    v_dx_c = dX.c

    wait("a", dX, v_dx_c)
    v_siluc = op("a", "activation", siluc[0:2, :], modrows[0:2, 0:D], AF.Silu, inc=sA)
    wait("t", sA, v_siluc)
    wait("t", sG, v_gconst)
    for cc in range(NC8):
        mm(pb[0][0:128, cc * 2:cc * 2 + 2], siluc[0:2, cc * 128:(cc + 1) * 128],
           ident[0:2, 0:2], True, True, inc=(sP if cc == NC8 - 1 else None))
    wait("v", sP, sP.c)
    v_ctcp = op("v", "tensor_copy", siluct[:], pb[0][0:128, 0:2 * NC8], inc=sV)

    NMP = 6 * D // 512
    v_modevac = [0] * NMP
    v_modmm = [0] * NMP
    wait("t", sV, v_ctcp)
    wmod_v = wmod_d.rearrange("(c p) e -> p c e", p=128)
    for j in range(NMP):
        if j >= 2:
            wait("s", sP, v_modmm[j - 2])
        vld = dma("s", wmod_sb[:, j % 2, :, :], wmod_v[:, :, j * 512:(j + 1) * 512], dX)
        wait("t", dX, vld)
        if j >= 2:
            wait("t", sV, v_modevac[j - 2])
        bank = pb[1 + (j % 2)]
        for cc in range(NC8):
            v_modmm[j] = mm(bank[0:2, :], siluct[:, cc * 2:(cc + 1) * 2],
                            wmod_sb[:, j % 2, cc, :], cc == 0, cc == NC8 - 1,
                            inc=(sP if cc == NC8 - 1 else None))
        wait("v", sP, sP.c)
        v_modevac[j] = op("v", "tensor_copy", modrows[0:2, j * 512:(j + 1) * 512],
                          bank[0:2, :], inc=sV)
    v_modrows = sV.c

    wait("t", sV, v_modrows)
    for v6 in range(6):
        for cc in range(NC8):
            i = v6 * NC8 + cc
            mm(pb[3][0:128, i * 2:i * 2 + 2],
               modrows[0:2, v6 * D + cc * 128:v6 * D + (cc + 1) * 128],
               ident[0:2, 0:2], True, True, inc=(sP if i == 47 else None))
    wait("v", sP, sP.c)
    for v6 in range(6):
        for cc in range(NC8):
            i = v6 * NC8 + cc
            for b in range(B):
                op("v", "tensor_copy", vcol(b, v6, cc), pb[3][0:128, i * 2 + b:i * 2 + b + 1])
    op("v", "memset", vnp[0:1, 0:1], 0.0, inc=sV)
    wait("v", sV, sV.c)

    for b in range(B):
        op("v", "tensor_scalar", vcols(b, 1), vcols(b, 1), 1.0, None, ALU.add)
        op("v", "tensor_scalar", vcols(b, 4), vcols(b, 4), 1.0, None, ALU.add)
        for v6 in (2, 5):
            g0 = vcols(b, v6)
            op("v", "tensor_scalar", sc_f1[:, 0:NC8], g0, 0.0, 2.0, ALU.is_ge, ALU.mult)
            op("v", "tensor_scalar", sc_f1[:, 0:NC8], sc_f1[:, 0:NC8], -1.0, None, ALU.add)
            op("v", "tensor_scalar", sc_f2[:, 0:NC8], g0, -1.0, None, ALU.mult)
            op("v", "tensor_max", sc_f2[:, 0:NC8], sc_f2[:, 0:NC8], g0)
            op("v", "tensor_scalar", sc_f2[:, 0:NC8], sc_f2[:, 0:NC8], 0.25, None, ALU.max)
            op("v", "tensor_mul", g0, sc_f1[:, 0:NC8], sc_f2[:, 0:NC8])
    v_modpost = op("v", "memset", vnp[0:1, 0:1], 0.0, inc=sV)
    if "d_mod" in dbg_d:
        wait("s", sV, v_modpost)
        dma("s", dbg_d["d_mod"], modv[:], dS)


    UN = 1024
    wq_units = []
    for ch in range(D // 128):
        for qt in range(DH // UN):
            wq_units.append((w1t_d[ch * 128:(ch + 1) * 128, qt * UN:(qt + 1) * UN],
                             w1q_d[ch * 128:(ch + 1) * 128, qt * UN:(qt + 1) * UN], 0))
    for ch in range(DH // 128):
        wq_units.append((w2t_d[ch * 128:(ch + 1) * 128, :],
                         w2q_d[ch * 128:(ch + 1) * 128, :], 1))
    NU = len(wq_units)
    assert NU == 64

    def wq_par(i):
        return dWa if i % 2 == 0 else dWb

    wq_fin = {}

    def emit_wq_finalize():
        op("v", "tensor_reduce", sc_f1[:, 0:1], wsum[:, 0:32], AX.X, ALU.add)
        v_ws = op("v", "tensor_reduce", sc_f1[:, 1:2], wsum[:, 32:64], AX.X, ALU.add,
                  inc=sV)
        wait("t", sV, v_ws)
        v_m0 = mm(pb[0][0:1, 0:2], ones_col[:, :], sc_f1[:, 0:2], True, True, inc=sP)
        wait("v", sP, v_m0)
        op("v", "tensor_scalar", scal11[0:1, 0:2], pb[0][0:1, 0:2], 1.0 / (D * DH),
           None, ALU.mult)
        op("v", "tensor_scalar", scal11[0:1, 0:2], scal11[0:1, 0:2], 1e-5, None, ALU.max)
        v_sw = op("v", "reciprocal", scal11[0:1, 2:4], scal11[0:1, 0:2], inc=sV)
        wait("t", sV, v_sw)
        v_m1 = mm(pb[1][:, 0:4], ones_row[0:1, :], scal11[0:1, 0:4], True, True, inc=sP)
        wait("v", sP, v_m1)
        op("v", "tensor_copy", swc[:, 0:1], pb[1][:, 2:3])
        op("v", "tensor_copy", swc[:, 1:2], pb[1][:, 0:1])
        op("v", "tensor_copy", swc[:, 2:3], pb[1][:, 3:4])
        op("v", "tensor_copy", swc[:, 3:4], pb[1][:, 1:2])
        op("v", "tensor_scalar", swn[:, 0:1], swc[:, 0:1], -1.0, None, ALU.mult)
        wq_fin["v"] = op("v", "tensor_scalar", swn[:, 1:2], swc[:, 2:3], -1.0, None,
                         ALU.mult, inc=sV)
        wq_fin["sem"] = sV.h

    wq1 = {"i": 0, "ld": [0] * NU, "red": [0] * NU}
    wq2 = {"i": 0, "ld": [0] * NU, "sp": [0] * NU, "tern": [0] * NU}

    def _wq_load(st, i):
        if st["ld"][i] == 0:
            if i >= 4:
                wait("s", sV, st["red"][i - 4] if "red" in st else st["tern"][i - 4])
            st["ld"][i] = dma("s", wqg_s[:, i % 4, :], wq_units[i][0], wq_par(i))

    def wq_p1(k):
        for i in range(wq1["i"], min(wq1["i"] + k, NU)):
            for pf in range(i, min(i + 4, NU)):
                _wq_load(wq1, pf)
            wait("v", wq_par(i), wq1["ld"][i])
            wq1["red"][i] = op("v", "tensor_reduce", wsum[:, i:i + 1],
                               wqg_s[:, i % 4, :], AX.X, ALU.add,
                               apply_absolute_value=True, inc=sV)
        wq1["i"] = min(wq1["i"] + k, NU)

    def wq_p2(k):
        for i in range(wq2["i"], min(wq2["i"] + k, NU)):
            for pf in range(i, min(i + 4, NU)):
                if wq2["ld"][pf] == 0:
                    if pf >= 4:
                        wait("s", sV, wq2["tern"][pf - 4])
                    else:
                        wait("s", sV, wq1["red"][NU - 4 + pf])
                    wq2["ld"][pf] = dma("s", wqg_s[:, pf % 4, :], wq_units[pf][0],
                                        wq_par(pf))
            wait("v", wq_par(i), wq2["ld"][i])
            if i >= 2:
                wait("v", dSw, wq2["sp"][i - 2])
            w12 = wq_units[i][2]
            swcol = swc[:, 0:1] if w12 == 0 else swc[:, 2:3]
            swneg = swn[:, 0:1] if w12 == 0 else swn[:, 1:2]
            for hf in range(0, UN, TBA):
                w_ = wqg_s[:, i % 4, hf:hf + TBA]
                op("v", "tensor_scalar", wqg_c[:, 0, :], w_, swcol, 0.5, ALU.mult,
                   ALU.is_ge)
                op("v", "tensor_scalar", wqg_c[:, 1, :], w_, swneg, 0.5, ALU.mult,
                   ALU.is_gt)
                op("v", "tensor_sub", wqg_q[:, i % 2, hf:hf + TBA], wqg_c[:, 0, :],
                   wqg_c[:, 1, :])
            wq2["tern"][i] = op("v", "memset", vnp[0:1, 2:3], 0.0, inc=sV)
            wait("s", sV, wq2["tern"][i])
            wq2["sp"][i] = dma("s", wq_units[i][1], wqg_q[:, i % 2, :], dSw)
        wq2["i"] = min(wq2["i"] + k, NU)


    def absmax_rows(nsub, v_src, dst_row=None):
        if dst_row is None:
            dst_row = mrow
        wait("t", sV, v_src)
        for sub in range(nsub):
            op("t", "transpose", pb[4][:, sub * 128:(sub + 1) * 128],
               mxt[:, sub * 128:(sub + 1) * 128], ident[:, :],
               inc=(sP if sub == nsub - 1 else None))
        wait("v", sP, sP.c)
        for sub in range(nsub):
            op("v", "tensor_reduce", m4[:, sub:sub + 1], pb[4][:, sub * 128:(sub + 1) * 128],
               AX.X, ALU.max, inc=(sV if sub == nsub - 1 else None))
        wait("s", sV, sV.c)
        for sub in range(nsub):
            vda = dma("s", dst_row[0:1, sub * 128:(sub + 1) * 128], m4[:, sub:sub + 1],
                      dX, allow_slow_non_contiguous=True)
        wait("v", dX, vda)
        return vda

    def ln_block(b, src_sb, dst_write, v6c, v6s, j):
        v_smm = [0] * NC8
        for cc in range(NC8):
            sqb = sc_f1 if cc % 2 == 0 else sc_f2
            if cc >= 2:
                wait("a", sP, v_smm[cc - 2])
            va = op("a", "activation", sqb[:], src_sb[:, cc, :], AF.Square, inc=sA)
            wait("t", sA, va)
            mm(pb[2][0:1, :], ones_col[:, :], src_sb[:, cc, :], cc == 0, cc == NC8 - 1)
            v_smm[cc] = mm(pb[3][0:1, :], ones_col[:, :], sqb[:], cc == 0, cc == NC8 - 1,
                           inc=sP)
        wait("v", sP, sP.c)
        op("v", "tensor_scalar", mrow[0:1, :], pb[2][0:1, :], 1.0 / D, None, ALU.mult)
        op("v", "tensor_scalar", msrow[0:1, :], pb[3][0:1, :], 1.0 / D, None, ALU.mult)
        op("v", "tensor_mul", invrow[0:1, :], mrow[0:1, :], mrow[0:1, :])
        v_var = op("v", "tensor_sub", invrow[0:1, :], msrow[0:1, :], invrow[0:1, :], inc=sV)
        wait("a", sV, v_var)
        v_ln = op("a", "activation", msrow[0:1, :], invrow[0:1, :], AF.Ln, bias=epscol[0:1, 0:1], inc=sA)
        wait("a", sA, v_ln)
        v_iv = op("a", "activation", invrow[0:1, :], msrow[0:1, :], AF.Exp, scale=-0.5,
                  inc=sA)
        wait("v", sA, v_iv)
        v_mi = op("v", "scalar_tensor_tensor", mirow[0:1, :], mrow[0:1, :], -1.0,
                  invrow[0:1, :], ALU.mult, ALU.mult, inc=sV)
        wait("t", sV, v_mi)
        pI = pb[6]
        pM = pb[7]
        mm(pI[:, :], ones_row[0:1, :], invrow[0:1, :], True, True)
        mm(pM[:, :], ones_row[0:1, :], mirow[0:1, :], True, True, inc=sP)
        wait("v", sP, sP.c)
        v_hc = 0
        for cc in range(NC8):
            op("v", "tensor_tensor", sc_f2[:], src_sb[:, cc, :], pI[:, :], ALU.mult)
            op("v", "tensor_tensor", sc_f2[:], sc_f2[:], pM[:, :], ALU.add)
            v_hc = op("v", "tensor_scalar", dst_write(cc), sc_f2[:], vcol(b, v6c, cc),
                      vcol(b, v6s, cc), ALU.mult, ALU.add, inc=sV)
        return v_hc

    def phase_A(b):
        guard_all()
        for j in range(NTA):
            t0 = j * TBA
            vld = dma("s", xt_sb[:], xv(b)[:, :, t0:t0 + TBA], dX)
            wait("a", dX, vld)
            wait("t", dX, vld)
            v_h = ln_block(b, xt_sb,
                           lambda cc, t0=t0: h_sb[:, cc, t0:t0 + TBA], 1, 0, j)
            if "d_h" in dbg_d:
                wait("s", sV, v_h)
                dma("s", dbg_d["d_h"][b, j], h_sb[:, :, t0:t0 + TBA], dS)
            wait("s", sV, v_h)

        ksqp = (sc_b1[:, :, :].rearrange("p a b -> p (a b)"),
                sc_b2[:, :, :].rearrange("p a b -> p (a b)"))
        for si, (Wd, Out, isq) in enumerate(((wk_d, K_sb, 0), (wq_d, Q_sb, 1))):
            if b == 0 and si == 1:
                emit_wq_finalize()
            wait("g", sP, sP.c)
            wait("g", sV, sV.c)
            v_w = dma("g", wk_sb[:], Wd.rearrange("(c p) e -> p c e", p=128), dW)
            wait("t", dW, v_w)
            jobs = [(j, ec) for j in range(NTA) for ec in range(NC8)]
            n = len(jobs)
            v_proj = [0] * n; v_cp = [0] * n; v_sq = [0] * n; v_nm = [0] * n
            v_l2e = [0] * n; v_nr = [0] * n; v_rep = [0] * n; v_fin = [0] * n

            def t_proj(u):
                j, ec = jobs[u]
                pbank = pb[3] if u % 2 == 0 else pb[2]
                if u >= 2:
                    wait("t", sA, v_sq[u - 2])
                for cc in range(NC8):
                    mm(pbank[:, :], wk_sb[:, cc, ec * 128:(ec + 1) * 128],
                       h_sb[:, cc, jobs[u][0] * TBA:jobs[u][0] * TBA + TBA],
                       cc == 0, cc == NC8 - 1, inc=(sP if cc == NC8 - 1 else None))
                v_proj[u] = sP.c

            def a_copy_sq(u):
                j, ec = jobs[u]
                t0 = j * TBA
                pbank = pb[3] if u % 2 == 0 else pb[2]
                wait("a", sP, v_proj[u])
                if u >= 2:
                    wait("a", sP, v_nm[u - 2])
                v_cp[u] = op("a", "activation", Out[:, ec, t0:t0 + TBA], pbank[:, :],
                             AF.Identity, inc=sA)
                v_sq[u] = op("a", "activation", ksqp[u % 2], Out[:, ec, t0:t0 + TBA],
                             AF.Square, inc=sA)

            def t_norm(u):
                nbank = pb[5] if u % 2 == 0 else pb[4]
                wait("t", sA, v_sq[u])
                if u >= 2:
                    wait("t", sV, v_fin[u - 2])
                v_nm[u] = mm(nbank[0:2, :], pair_ones[:, :], ksqp[u % 2], True, True,
                             inc=sP)

            def a_lnexp(u):
                nbank = pb[5] if u % 2 == 0 else pb[4]
                wait("a", sP, v_nm[u])
                if u >= 2:
                    wait("a", sP, v_rep[u - 2])
                op("a", "activation", ninv[0:2, u % 2, :], nbank[0:2, :], AF.Ln)
                v_l2e[u] = op("a", "activation", ninv[0:2, u % 2, :],
                              ninv[0:2, u % 2, :], AF.Exp, scale=-0.5, inc=sA)

            def v_minc(u):
                wait("v", sA, v_l2e[u])
                op("v", "tensor_scalar", ninv[0:2, u % 2, :], ninv[0:2, u % 2, :], 1e12,
                   None, ALU.min)
                if isq:
                    op("v", "tensor_scalar", ninv[0:2, u % 2, :], ninv[0:2, u % 2, :],
                       0.125, None, ALU.mult)
                v_nr[u] = op("v", "memset", vnp[0:1, 0:1], 0.0, inc=sV)

            def t_rep(u):
                nbank = pb[5] if u % 2 == 0 else pb[4]
                wait("t", sV, v_nr[u])
                wait("t", sA, v_l2e[u])
                v_rep[u] = mm(nbank[:, :], pairsel[:, :], ninv[0:2, u % 2, :], True, True,
                              inc=sP)

            def v_fin_(u):
                j, ec = jobs[u]
                t0 = j * TBA
                nbank = pb[5] if u % 2 == 0 else pb[4]
                wait("v", sP, v_rep[u])
                wait("v", sA, v_cp[u])
                v_fin[u] = op("v", "tensor_tensor", Out[:, ec, t0:t0 + TBA],
                              Out[:, ec, t0:t0 + TBA], nbank[:, :], ALU.mult, inc=sV)

            for u in range(n):
                t_proj(u)
                a_copy_sq(u)
                if u >= 1:
                    t_norm(u - 1)
                    a_lnexp(u - 1)
                    v_minc(u - 1)
                if u >= 2:
                    t_rep(u - 2)
                    v_fin_(u - 2)
                if b == 0:
                    wq_p1(2) if si == 0 else wq_p2(1)
            t_norm(n - 1)
            a_lnexp(n - 1)
            v_minc(n - 1)
            t_rep(n - 2)
            v_fin_(n - 2)
            t_rep(n - 1)
            v_fin_(n - 1)

        wait("g", sP, sP.c)
        wait("g", sV, sV.c)
        v_w = dma("g", wk_sb[:], wv_d.rearrange("(c p) e -> p c e", p=128), dW)
        wait("t", dW, v_w)
        for tt in range(NTT):
            tj, tt4 = divmod(tt, TBA // 128)
            t0 = tj * TBA
            for half in range(2):
                wait("t", sA, sA.c)
                for cc in range(NC8):
                    mm(pb[4][:, :], h_sb[:, cc, tt * 128:(tt + 1) * 128],
                       wk_sb[:, cc, half * 512:(half + 1) * 512], cc == 0, cc == NC8 - 1,
                       inc=(sP if cc == NC8 - 1 else None))
                wait("a", sP, sP.c)
                op("a", "activation", V_sb[:, tt, half * 512:(half + 1) * 512],
                   pb[4][:, :], AF.Identity, inc=sA)
            if b == 0:
                wq_p2(1)
        wait("g", sA, sA.c)
        v_w = dma("g", wbeta_sb[:], wb_d.rearrange("(c p) e -> p c e", p=128), dW)
        wait("t", dW, v_w)
        for tt in range(NTT):
            wait("t", sA, sA.c)
            for cc in range(NC8):
                mm(pb[2][:, 0:NH], h_sb[:, cc, tt * 128:(tt + 1) * 128],
                   wbeta_sb[:, cc, :], cc == 0, cc == NC8 - 1,
                   inc=(sP if cc == NC8 - 1 else None))
            wait("a", sP, sP.c)
            v_be = op("a", "activation", beta_sb[:, tt, :], pb[2][:, 0:NH], AF.Exp,
                      scale=-1.0, inc=sA)
            wait("v", sA, v_be)
            op("v", "tensor_scalar", beta_sb[:, tt, :], beta_sb[:, tt, :], 1.0, None,
               ALU.add)
            op("v", "reciprocal", beta_sb[:, tt, :], beta_sb[:, tt, :], inc=sV)
            if b == 0:
                wq_p2(1)
        if "d_k" in dbg_d:
            wait("s", sV, sV.c)
            dma("s", dbg_d["d_k"], K_sb[:], dS)
            dma("s", dbg_d["d_q"], Q_sb[:], dS)
        if "d_v" in dbg_d:
            wait("s", sA, sA.c)
            wait("s", sV, sV.c)
            dma("s", dbg_d["d_v"], V_sb[:], dS)
            dma("s", dbg_d["d_beta"], beta_sb[:], dS)

    def phase_B(b):
        if b == 0:
            wq_p1(NU)
            wq_p2(NU)
        guard_all()
        v_wo = dma("g", wo_sb[:], wo_d.rearrange("(c p) e -> p c e", p=128), dW)
        m_oev = [0]
        for p in range(NC8):
            e0 = p * 128
            for j in range(NTA):
                t0 = j * TBA
                wait("t", sA, m_oev[0])
                ks = list(range(4 * j, NTT))
                v_at_v = [0] * len(ks)
                v_at_a = [0] * len(ks)
                v_omm = [0] * len(ks)

                sbanks = ((pb[4], pb[5]), (pb[6], pb[7]), (pb[2], pb[3]))

                def emit_scores(k):
                    i = ks[k]
                    par = k % 3
                    if k >= 3:
                        wait("t", sV, v_at_v[k - 3])
                        wait("t", sA, v_at_a[k - 3])
                    mm(sbanks[par][0][:, :], K_sb[0:64, p, i * 128:i * 128 + 128],
                       Q_sb[0:64, p, t0:t0 + TBA], True, True, tp=(0, 0), inc=sP)
                    mm(sbanks[par][1][:, :], K_sb[64:128, p, i * 128:i * 128 + 128],
                       Q_sb[64:128, p, t0:t0 + TBA], True, True, tp=(64, 0), inc=sP)
                    return sP.c

                def emit_evac(k, v_sc):
                    i = ks[k]
                    par = k % 3
                    diag = i - 4 * j if (i - 4 * j) < 4 else None
                    banks = sbanks[par]
                    if diag is not None:
                        c0, c1 = diag * 128, (diag + 1) * 128
                        wait("v", sP, v_sc)
                        for hp, bank in ((0, banks[0]), (1, banks[1])):
                            op("v", "scalar_tensor_tensor", at3_sb[:, par, hp, :],
                               bank[:, :], 1.0, stair[:, diag, :],
                               ALU.mult, ALU.mult)
                            bcol = beta_sb[:, i, 2 * p + hp:2 * p + hp + 1]
                            op("v", "tensor_scalar", dscr[:, :], ident[:, :], bcol, None,
                               ALU.mult)
                            op("v", "tensor_add", at3_sb[:, par, hp, c0:c1],
                               at3_sb[:, par, hp, c0:c1], dscr[:, :], inc=sV)
                        v_at_v[k] = sV.c
                        v_at_a[k] = 0
                    else:
                        wait("a", sP, v_sc)
                        v_at_a[k] = op("a", "activation", at3_sb[:, par, 0, :],
                                       banks[0][:, :], AF.Identity, inc=sA)
                        wait("v", sP, v_sc)
                        v_at_v[k] = op("v", "tensor_copy", at3_sb[:, par, 1, :],
                                       banks[1][:, :], inc=sV)

                def emit_O(k):
                    i = ks[k]
                    par = k % 3
                    first = (k == 0)
                    last = (k == len(ks) - 1)
                    if v_at_v[k]:
                        wait("t", sV, v_at_v[k])
                    if v_at_a[k]:
                        wait("t", sA, v_at_a[k])
                    mm(pb[0][0:64, :], V_sb[:, i, e0:e0 + 64], at3_sb[:, par, 0, :],
                       first, False, tp=(0, 0))
                    v_omm[k] = mm(pb[1][64:128, :], V_sb[:, i, e0 + 64:e0 + 128],
                                  at3_sb[:, par, 1, :], first, last, tp=(0, 64), inc=sP)

                for k in range(len(ks)):
                    v_sc = emit_scores(k)
                    emit_evac(k, v_sc)
                    if k >= 1:
                        emit_O(k - 1)
                emit_O(len(ks) - 1)
                wait("a", sP, sP.c)
                op("a", "activation", O_sb[0:64, p, t0:t0 + TBA], pb[0][0:64, :],
                   AF.Identity)
                m_oev[0] = op("a", "activation", O_sb[64:128, p, t0:t0 + TBA],
                              pb[1][64:128, :], AF.Identity, inc=sA)
        if "d_o" in dbg_d:
            wait("s", sA, sA.c)
            dma("s", dbg_d["d_o"][b], O_sb[:], dS)
        return dW.c

    def phase_C(b, v_wo):
        guard_all()
        wait("t", dW, v_wo)
        x1tp = (x1t_sb, x1t2_sb)
        mrowQ = rowsQ[0:1, 0, :]
        invQ = rowsQ[0:1, 1, :]
        miQ = rowsQ[0:1, 2, :]
        ypar1 = (scf3f, convbf)
        MC = [dict() for _ in range(NTA)]

        def ld_x(j):
            if j >= 1:
                wait("s", sV, MC[j - 1]["x1last"])
            MC[j]["ldx"] = dma("s", xtc_sb[:], xv(b)[:, :, j * TBA:(j + 1) * TBA], dX)

        def lnC_stats(j):
            src = x1tp[j % 2]
            v_smm = [0] * NC8
            for cc in range(NC8):
                sqb = sc_f1 if cc % 2 == 0 else sc_f2
                if cc == 0:
                    wait("a", sV, MC[j]["x1last"])
                if cc >= 2:
                    wait("a", sP, v_smm[cc - 2])
                va = op("a", "activation", sqb[:], src[:, cc, :], AF.Square, inc=sA)
                wait("t", sA, va)
                mm(pb[2][0:1, :], ones_col[:, :], src[:, cc, :], cc == 0, cc == NC8 - 1)
                v_smm[cc] = mm(pb[3][0:1, :], ones_col[:, :], sqb[:], cc == 0,
                               cc == NC8 - 1, inc=sP)
            wait("v", sP, sP.c)
            op("v", "tensor_scalar", mrow[0:1, :], pb[2][0:1, :], 1.0 / D, None,
               ALU.mult)
            op("v", "tensor_scalar", msrow[0:1, :], pb[3][0:1, :], 1.0 / D, None,
               ALU.mult)
            op("v", "tensor_mul", invrow[0:1, :], mrow[0:1, :], mrow[0:1, :])
            v_var = op("v", "tensor_sub", invrow[0:1, :], msrow[0:1, :],
                       invrow[0:1, :], inc=sV)
            wait("a", sV, v_var)
            v_ln = op("a", "activation", msrow[0:1, :], invrow[0:1, :], AF.Ln,
                      bias=epscol[0:1, 0:1], inc=sA)
            wait("a", sA, v_ln)
            v_iv = op("a", "activation", invrow[0:1, :], msrow[0:1, :], AF.Exp,
                      scale=-0.5, inc=sA)
            wait("v", sA, v_iv)
            v_mi = op("v", "scalar_tensor_tensor", msrow[0:1, :], mrow[0:1, :], -1.0,
                      invrow[0:1, :], ALU.mult, ALU.mult, inc=sV)
            wait("t", sV, v_mi)
            mm(pb[6][:, :], ones_row[0:1, :], invrow[0:1, :], True, True)
            v_rep = mm(pb[7][:, :], ones_row[0:1, :], msrow[0:1, :], True, True,
                       inc=sP)
            MC[j]["lnrep"] = v_rep

        def tail_pieces(j):
            t0 = j * TBA
            st = {"zk": [0] * NC8, "maxm": [0] * NC8, "hlast": 0}

            def hprod(cc):
                if cc == 0:
                    wait("v", sP, MC[j]["lnrep"])
                op("v", "tensor_tensor", habsC[:, 0, :], x1tp[j % 2][:, cc, :],
                   pb[6][:, :], ALU.mult)
                op("v", "tensor_tensor", habsC[:, 0, :], habsC[:, 0, :], pb[7][:, :],
                   ALU.add)
                st["hlast"] = op("v", "tensor_scalar", h2t_sb[:, cc, :],
                                 habsC[:, 0, :], vcol(b, 4, cc), vcol(b, 3, cc),
                                 ALU.mult, ALU.add, inc=sV)

            def absmax_piece():
                op("v", "memset", mxt[:], 0.0)
                v_absc = [0] * NC8
                for cc in range(NC8):
                    if cc == 0:
                        wait("a", sV, st["hlast"])
                    if cc >= 2:
                        wait("a", sV, st["maxm"][cc - 2])
                    v_absc[cc] = op("a", "activation", habsC[:, cc % 2, :],
                                    h2t_sb[:, cc, :], AF.Abs, inc=sA)
                    wait("v", sA, v_absc[cc])
                    st["maxm"][cc] = op("v", "tensor_max", mxt[:], mxt[:],
                                        habsC[:, cc % 2, :], inc=sV)
                absmax_rows(TBA // 128, sV.c, mrowQ)
                op("v", "tensor_scalar", mrowQ[0:1, :], mrowQ[0:1, :], 1e-5, None,
                   ALU.max)
                op("v", "reciprocal", invQ[0:1, :], mrowQ[0:1, :])
                op("v", "tensor_scalar", invQ[0:1, :], invQ[0:1, :], 127.0, None,
                   ALU.mult)
                op("v", "reciprocal", miQ[0:1, :], invQ[0:1, :])
                v_row = op("v", "tensor_scalar", miQ[0:1, :], miQ[0:1, :],
                           scal11[0:1, 0:1], None, ALU.mult, inc=sV)
                wait("t", sV, v_row)
                mm(pb[5][:, :], ones_row[0:1, :], invQ[0:1, :], True, True, inc=sP)
                wait("v", sP, sP.c)
                op("v", "tensor_copy", s_yrep[:], pb[5][:, :], inc=sV)

            def q1_y(k):
                return op("v", "tensor_mul", ypar1[k % 2],
                          h2t_sb[:, k, :], s_yrep[:, :], inc=sV)

            def q1_cast(k, v_y):
                wait("a", sV, v_y)
                op("a", "activation", qiC[:], ypar1[k % 2], AF.Identity)
                return op("a", "activation", qzC[:, k % 2, :], qiC[:], AF.Identity,
                          inc=sA)

            def q1_ins(k, v_zf):
                wait("v", sA, v_zf)
                op("v", "tensor_sub", ypar1[k % 2], ypar1[k % 2], qzC[:, k % 2, :])
                op("v", "scalar_tensor_tensor", scb2f, ypar1[k % 2], -0.5,
                   qzC[:, k % 2, :], ALU.is_lt, ALU.subtract)
                return op("v", "scalar_tensor_tensor", h2q_res[:, k, t0:t0 + TBA],
                          ypar1[k % 2], 0.5, scb2f, ALU.is_gt, ALU.subtract, inc=sV)

            def quant_piece(ks, last):
                for k in ks:
                    st["zk"][k] = q1_cast(k, q1_y(k))
                    if k >= 1:
                        q1_ins(k - 1, st["zk"][k - 1])
                if last:
                    q1_ins(NC8 - 1, st["zk"][NC8 - 1])
                    wait("s", sV, sV.c)
                    dma("s", sc1_d[b, t0:t0 + TBA].unsqueeze(0), miQ[0:1, :], dS)

            return [lambda: (hprod(0), hprod(1)),
                    lambda: (hprod(2), hprod(3)),
                    lambda: (hprod(4), hprod(5)),
                    lambda: (hprod(6), hprod(7)),
                    lambda: absmax_piece(),
                    lambda: quant_piece((0, 1, 2), False),
                    lambda: quant_piece((3, 4, 5), False),
                    lambda: quant_piece((6, 7), True)]

        def emit_wout(j, pieces):
            t0 = j * TBA
            xt = x1tp[j % 2]
            v_x1c = [0] * NC8
            for dc in range(NC8):
                pbank = pb[dc % 2]
                if dc == 0 and j >= 2:
                    wait("v", dS, MC[j - 2]["spill"])
                if dc >= 2:
                    wait("t", sV, v_x1c[dc - 2])
                for ec in range(NC8):
                    mm(pbank[:, :], wo_sb[:, ec, dc * 128:(dc + 1) * 128],
                       O_sb[:, ec, t0:t0 + TBA], ec == 0, ec == NC8 - 1,
                       inc=(sP if ec == NC8 - 1 else None))
                wait("v", sP, sP.c)
                if dc == 0:
                    wait("v", dX, MC[j]["ldx"])
                v_x1c[dc] = op("v", "scalar_tensor_tensor", xt[:, dc, :], pbank[:, :],
                               vcol(b, 2, dc), xtc_sb[:, dc, :], ALU.mult, ALU.add,
                               inc=sV)
                if pieces is not None:
                    pieces[dc]()
            MC[j]["x1last"] = v_x1c[NC8 - 1]
            wait("s", sV, v_x1c[NC8 - 1])
            MC[j]["spill"] = dma("s", x1v(b)[:, :, t0:t0 + TBA], xt[:], dS)
            if j + 1 < NTA:
                ld_x(j + 1)

        ld_x(0)
        emit_wout(0, None)
        for j in range(1, NTA):
            lnC_stats(j - 1)
            emit_wout(j, tail_pieces(j - 1))
        lnC_stats(NTA - 1)
        for pc in tail_pieces(NTA - 1):
            pc()

    def phase_D(b):
        guard_all()
        wait("s", dSw, dSw.c)
        w1v = w1q_d.rearrange("(c p) e -> p c e", p=128)
        vw1a = dma("s", w1q_sb[:, 0:NC8 // 2, :], w1v[:, 0:NC8 // 2, :], dWa)
        vw1 = dma("s", w1q_sb[:, NC8 // 2:NC8, :], w1v[:, NC8 // 2:NC8, :], dW)
        w2v = w2q_d.rearrange("(c p) e -> p c e", p=128)

        NK = NCH // 2
        x1dv = sc_f3
        osclv = sc_f2[:, 0:TBD]
        habs = (sc_f1[:, 0:TBD], sc_f1[:, TBD:2 * TBD])
        ypar2 = (scf3f, convbf)
        syb = s_yrep[:, 0:TBD].unsqueeze(1).broadcast_to([128, 2, TBD])
        M = [dict() for _ in range(NTD)]

        def srow(j):
            return ninv[0:1, j % 2, 0:TBD + 2]

        def scrow_load(j):
            t0 = j * TBD
            lo = max(t0 - 1, 0)
            hi = min(t0 + TBD + 1, T)
            padL = 1 if j == 0 else 0
            if j >= 2:
                wait("s", sP, M[j - 2]["toprep"])
            M[j]["scld"] = dma("s", srow(j)[0:1, padL:padL + (hi - lo)],
                               sc1_d[b, lo:hi].unsqueeze(0), dX)

        def w2s01_load(j):
            if j >= 1:
                wait("s", sP, M[j - 1]["w2mm"])
            M[j]["w2s01"] = ((dW, dma("s", w2s_sb[:, 0, :, :], w2v[:, :, 0:128], dW)),
                             (dWb, dma("s", w2s_sb[:, 1, :, :], w2v[:, :, 128:256], dWb)))

        def emit_top(j):
            wait("v", dX, M[j]["scld"])
            if j == 0:
                op("v", "memset", srow(j)[0:1, 0:1], 0.0)
            if j == NTD - 1:
                op("v", "memset", srow(j)[0:1, TBD + 1:TBD + 2], 0.0)
            if j == 0 or j == NTD - 1:
                ecol = 0 if j == 0 else TBD + 1
                op("v", "memset", h1s_sb[:, :, ecol:ecol + 1], 0.0)
            v_e = op("v", "memset", vnp[0:1, 0:1], 0.0, inc=sV)
            M[j]["edge"] = v_e
            wait("t", sV, v_e)
            M[j]["toprep"] = mm(pb[7][:, 0:TBD + 2], ones_row[0:1, :], srow(j),
                                True, True, inc=sP)
            wait("v", sP, sP.c)
            op("v", "tensor_copy", s_orep[:, 0:TBD + 2], pb[7][:, 0:TBD + 2], inc=sV)

        def q2_y(k):
            return op("v", "tensor_mul",
                      ypar2[k % 2].rearrange("p (a b) -> p a b", a=2),
                      hs_sb[:, 2 * k:2 * k + 2, :], syb, inc=sV)

        def q2_cast(k, v_y):
            wait("a", sV, v_y)
            op("a", "activation", qiD[:], ypar2[k % 2], AF.Identity)
            return op("a", "activation", qzD[:, k % 2, :], qiD[:], AF.Identity,
                      inc=sA)

        def q2_ins(k, v_zf):
            wait("v", sA, v_zf)
            op("v", "tensor_sub", ypar2[k % 2], ypar2[k % 2], qzD[:, k % 2, :])
            op("v", "scalar_tensor_tensor", scb2f, ypar2[k % 2], -0.5,
               qzD[:, k % 2, :], ALU.is_lt, ALU.subtract)
            return op("v", "scalar_tensor_tensor",
                      q2o_sb[:, 2 * k:2 * k + 2, 0:TBD],
                      ypar2[k % 2].rearrange("p (a b) -> p a b", a=2), 0.5,
                      scb2f.rearrange("p (a b) -> p a b", a=2),
                      ALU.is_gt, ALU.subtract, inc=sV)

        def emit_H1(j, qj):
            t0 = j * TBD
            lo = max(t0 - 1, 0)
            hi = min(t0 + TBD + 1, T)
            pL = 1 if j == 0 else 0
            pR = 1 if j == NTD - 1 else 0
            W = TBD + 2 - pL - pR
            v_ec = [0] * NCH
            vy = [0] * NK
            vz = [0] * NK
            for cc in range(NCH):
                pbank = pb[cc % 2]
                if cc == 0:
                    if j == 0:
                        wait("t", dW, vw1)
                        wait("t", dWa, vw1a)
                    else:
                        wait("t", sV, M[j - 1]["convv"])
                        wait("t", sA, M[j - 1]["tapa"])
                    wait("t", sV, M[j]["edge"])
                if cc >= 2:
                    wait("t", sV, v_ec[cc - 2])
                for dc in range(NC8):
                    mm(pbank[:, pL:pL + W], w1q_sb[:, dc, cc * 128:(cc + 1) * 128],
                       h2q_res[:, dc, lo:hi], dc == 0, dc == NC8 - 1,
                       inc=(sP if dc == NC8 - 1 else None))
                wait("v", sP, sP.c)
                v_ec[cc] = op("v", "scalar_tensor_tensor", h1s_sb[:, cc, pL:pL + W],
                              pbank[:, pL:pL + W], 1.0, s_orep[:, pL:pL + W],
                              ALU.mult, ALU.mult, inc=sV)
                if qj is not None and cc % 2 == 1:
                    k = cc // 2
                    vy[k] = q2_y(k)
                    vz[k] = q2_cast(k, vy[k])
                    if k >= 1:
                        q2_ins(k - 1, vz[k - 1])
            if qj is not None:
                q2_ins(NK - 1, vz[NK - 1])
                M[qj]["q2"] = sV.c
            M[j]["vec"] = v_ec

        def emit_quant_tail(qj):
            vy = [0] * NK
            vz = [0] * NK
            for k in range(NK):
                vy[k] = q2_y(k)
                vz[k] = q2_cast(k, vy[k])
                if k >= 1:
                    q2_ins(k - 1, vz[k - 1])
            q2_ins(NK - 1, vz[NK - 1])
            M[qj]["q2"] = sV.c

        def emit_W2conv(jw, cj):
            if cj is not None:
                vec = M[cj]["vec"]
                v_tap = [0] * NCH
                v_cvk = [0] * NCH
                v_abs = [0] * NCH
                v_max = [0] * NCH

                def tap0(cc):
                    if cc == 0:
                        if cj >= 1:
                            wait("a", sV, M[cj - 1]["q2"])
                    wait("a", sV, vec[cc])
                    v_tap[cc] = op("a", "activation", convb[:, cc % 2, :],
                                   h1s_sb[:, cc, 0:TBD], AF.Identity,
                                   scale=cw_sb[:, cc * 3:cc * 3 + 1], inc=sA)

                def conv_step(cc):
                    if cc == 0:
                        tap0(0)
                        op("v", "memset", mxt[:], 0.0)
                    if cc + 1 < NCH:
                        tap0(cc + 1)
                    cb_ = convb[:, cc % 2, :]
                    wait("v", sA, v_tap[cc])
                    op("v", "scalar_tensor_tensor", cb_, h1s_sb[:, cc, 1:TBD + 1],
                       cw_sb[:, cc * 3 + 1:cc * 3 + 2], cb_, ALU.mult, ALU.add)
                    v_cvk[cc] = op("v", "scalar_tensor_tensor", cb_,
                                   h1s_sb[:, cc, 2:TBD + 2],
                                   cw_sb[:, cc * 3 + 2:cc * 3 + 3], cb_,
                                   ALU.mult, ALU.add, inc=sV)
                    wait("a", sV, v_cvk[cc])
                    op("a", "activation", hs_sb[:, cc, :], convb[:, cc % 2, :],
                       AF.Silu, bias=cb_sb[:, cc:cc + 1])
                    if cc >= 2:
                        wait("a", sV, v_max[cc - 2])
                    v_abs[cc] = op("a", "activation", habs[cc % 2],
                                   hs_sb[:, cc, :], AF.Abs, inc=sA)
                    wait("v", sA, v_abs[cc])
                    v_max[cc] = op("v", "tensor_max", mxt[:, (cc % 2) * TBD:
                                   (cc % 2) * TBD + TBD],
                                   mxt[:, (cc % 2) * TBD:(cc % 2) * TBD + TBD],
                                   habs[cc % 2], inc=sV)

            if jw is not None:
                t0w = jw * TBD
                vws = [0] * NC8
                vws[0], vws[1] = M[jw]["w2s01"]
                v_oc = [0] * NC8
            for dc in range(NC8):
                if jw is not None:
                    if dc == 0:
                        wait("s", sV, M[jw]["q2"])
                    if dc >= 2:
                        wait("s", sV, v_oc[dc - 2])
                    vxs = dma("s", x1dv[:, dc % 2, :],
                              x1v(b)[:, dc, t0w:t0w + TBD], dX)
                    pbank = pb[2 + dc % 2]
                    if dc == 0:
                        wait("t", sV, M[jw]["q2"])
                        if jw >= 1:
                            wait("v", dS, M[jw - 1]["store"])
                    if dc >= 2:
                        wait("t", sV, v_oc[dc - 2])
                    wait("t", vws[dc][0], vws[dc][1])
                    for cc in range(NCH):
                        vmm = mm(pbank[:, 0:TBD], w2s_sb[:, dc % 2, cc, :],
                                 q2o_sb[:, cc, 0:TBD], cc == 0, cc == NCH - 1,
                                 inc=(sP if cc == NCH - 1 else None))
                    if dc + 2 < NC8:
                        wait("s", sP, sP.c)
                        q_ = dW if dc % 2 == 0 else dWb
                        vws[dc + 2] = (q_, dma("s", w2s_sb[:, dc % 2, :, :],
                                          w2v[:, :, (dc + 2) * 128:(dc + 3) * 128], q_))
                    else:
                        M[jw]["w2mm"] = vmm
                    wait("v", sP, sP.c)
                    wait("v", dX, vxs)
                    op("v", "scalar_tensor_tensor", sc_f4[:], pbank[:, 0:TBD], 1.0,
                       osclv, ALU.mult, ALU.mult)
                    v_oc[dc] = op("v", "scalar_tensor_tensor", outd_sb[:, dc, :],
                                  sc_f4[:], vcol(b, 5, dc), x1dv[:, dc % 2, :],
                                  ALU.mult, ALU.add, inc=sV)
                    if dc == 3 and jw + 2 < NTD:
                        scrow_load(jw + 2)
                if cj is not None:
                    for q in range(4):
                        conv_step(dc * 4 + q)
            if cj is not None:
                M[cj]["convv"] = v_cvk[NCH - 1]
                M[cj]["tapa"] = v_tap[NCH - 1]
            if jw is not None:
                if jw + 1 < NTD:
                    w2s01_load(jw + 1)
                wait("s", sV, v_oc[NC8 - 1])
                M[jw]["store"] = dma("s", outv(b)[:, :, t0w:t0w + TBD], outd_sb[:], dS)

        def emit_absfin(j):
            v_hs = op("v", "tensor_max", mxt[:, 0:TBD], mxt[:, 0:TBD],
                      mxt[:, TBD:2 * TBD], inc=sV)
            absmax_rows(TBD // 128, v_hs)
            op("v", "tensor_scalar", mrow[0:1, 0:TBD], mrow[0:1, 0:TBD], 1e-5, None,
               ALU.max)
            op("v", "reciprocal", invrow[0:1, 0:TBD], mrow[0:1, 0:TBD])
            op("v", "tensor_scalar", invrow[0:1, 0:TBD], invrow[0:1, 0:TBD], 127.0,
               None, ALU.mult)
            op("v", "reciprocal", mirow[0:1, 0:TBD], invrow[0:1, 0:TBD])
            v_row = op("v", "tensor_scalar", mirow[0:1, 0:TBD], mirow[0:1, 0:TBD],
                       scal11[0:1, 1:2], None, ALU.mult, inc=sV)
            wait("t", sV, v_row)
            mm(pb[5][:, 0:TBD], ones_row[0:1, :], invrow[0:1, 0:TBD], True, True)
            mm(pb[6][:, 0:TBD], ones_row[0:1, :], mirow[0:1, 0:TBD], True, True,
               inc=sP)
            wait("v", sP, sP.c)
            op("v", "tensor_copy", s_yrep[:, 0:TBD], pb[5][:, 0:TBD])
            op("v", "tensor_copy", osclv, pb[6][:, 0:TBD], inc=sV)

        scrow_load(0)
        scrow_load(1)
        w2s01_load(0)
        emit_top(0)
        emit_H1(0, None)
        emit_W2conv(None, 0)
        emit_absfin(0)
        for j in range(1, NTD):
            emit_top(j)
            emit_H1(j, j - 1)
            emit_W2conv(j - 1, j)
            emit_absfin(j)
        emit_quant_tail(NTD - 1)
        emit_W2conv(NTD - 1, None)

    for b in range(B):
        phase_A(b)
        v_wo = phase_B(b)
        phase_C(b, v_wo)
        phase_D(b)

    em("s", lambda e, h=dS.h: e.wait_ge(h, dS.c))
    em("g", lambda e, h=dS.h: e.wait_ge(h, dS.c))

    with nc.Block() as block:
        @block.tensor
        def _(t):
            for f in E["t"]:
                f(t)

        @block.vector
        def _(v):
            for f in E["v"]:
                f(v)

        @block.scalar
        def _(a):
            for f in E["a"]:
                f(a)

        @block.gpsimd
        def _(g):
            for f in E["g"]:
                f(g)

        @block.sync
        def _(s):
            for f in E["s"]:
                f(s)

    return nc


def _shards(L0_attn_x, L0_attn_Wk, L0_attn_Wv, L0_attn_Wq, L0_attn_Wbeta, L0_attn_Wout,
            L0_ffn_w1, L0_ffn_conv_w, L0_ffn_conv_b, L0_ffn_w2, L0_c, L0_wmod):
    f32 = np.float32
    com = {
        "wk": np.ascontiguousarray(L0_attn_Wk, dtype=f32),
        "wq": np.ascontiguousarray(L0_attn_Wq, dtype=f32),
        "wv": np.ascontiguousarray(L0_attn_Wv, dtype=f32),
        "wb": np.ascontiguousarray(L0_attn_Wbeta, dtype=f32),
        "wo": np.ascontiguousarray(L0_attn_Wout, dtype=f32),
        "w1t": np.ascontiguousarray(np.asarray(L0_ffn_w1, dtype=f32).T),
        "w2t": np.ascontiguousarray(np.asarray(L0_ffn_w2, dtype=f32).T),
        "cw": np.ascontiguousarray(
            np.asarray(L0_ffn_conv_w, dtype=f32).reshape(NCH, 128, 3)
            .transpose(1, 0, 2).reshape(128, 3 * NCH)),
        "cb": np.ascontiguousarray(
            np.asarray(L0_ffn_conv_b, dtype=f32).reshape(NCH, 128).T),
        "wmod": np.ascontiguousarray(L0_wmod, dtype=f32),
    }
    maps = []
    for i in range(8):
        m = dict(com)
        m["x"] = np.ascontiguousarray(
            np.asarray(L0_attn_x[2 * i:2 * i + 2], dtype=f32).transpose(0, 2, 1))
        m["cvec"] = np.ascontiguousarray(L0_c[2 * i:2 * i + 2], dtype=f32)
        maps.append(m)
    return maps


def kernel(**inputs):
    maps = _shards(**inputs)
    nc = build_program(DBG if 'DBG' in globals() else ())
    res = run_bass_kernel_spmd(nc, maps, core_ids=list(range(8)))
    out = np.concatenate([r["out"].transpose(0, 2, 1) for r in res.results], axis=0)
    return np.ascontiguousarray(out, dtype=np.float32)


DBG = ()


# version 16
# speedup vs baseline: 1.0012x; 1.0012x over previous
import numpy as np
from contextlib import ExitStack
import concourse.bass as bass
import concourse.mybir as mybir
from concourse.bass_utils import run_bass_kernel_spmd

F32 = mybir.dt.float32
BF16 = mybir.dt.bfloat16
I32 = mybir.dt.int32
ALU = mybir.AluOpType
AF = mybir.ActivationFunctionType
AX = mybir.AxisListType

B, T, D, NH, DM, DH = 2, 2048, 1024, 16, 64, 4096
NC8, NCH = 8, 32
TBA, TBD = 512, 256
NTA, NTD, NTT = T // TBA, T // TBD, T // 128
EPS = 1e-5
KB = 1024


class Ctr:
    def __init__(self, h):
        self.h = h
        self.c = 0

    def inc(self, n=1):
        self.c += n
        return self.c


def build_program(dbg=(), race_debug=False, self_sync=True, sync_engines=('v',), sync_lag=1, sync_mode='chain'):
    nc = bass.Bass("TRN2", target_bir_lowering=False,
                   detect_race_conditions=race_debug)

    x_d = nc.dram_tensor("x", [B, D, T], F32, kind="ExternalInput").ap()
    wk_d = nc.dram_tensor("wk", [D, D], F32, kind="ExternalInput").ap()
    wq_d = nc.dram_tensor("wq", [D, D], F32, kind="ExternalInput").ap()
    wv_d = nc.dram_tensor("wv", [D, D], F32, kind="ExternalInput").ap()
    wb_d = nc.dram_tensor("wb", [D, NH], F32, kind="ExternalInput").ap()
    wo_d = nc.dram_tensor("wo", [D, D], F32, kind="ExternalInput").ap()
    w1t_d = nc.dram_tensor("w1t", [D, DH], F32, kind="ExternalInput").ap()
    w2t_d = nc.dram_tensor("w2t", [DH, D], F32, kind="ExternalInput").ap()
    cw_d = nc.dram_tensor("cw", [128, 3 * NCH], F32, kind="ExternalInput").ap()
    cb_d = nc.dram_tensor("cb", [128, NCH], F32, kind="ExternalInput").ap()
    c_d = nc.dram_tensor("cvec", [B, D], F32, kind="ExternalInput").ap()
    wmod_d = nc.dram_tensor("wmod", [D, 6 * D], F32, kind="ExternalInput").ap()
    out_d = nc.dram_tensor("out", [B, D, T], F32, kind="ExternalOutput").ap()
    w1q_d = nc.dram_tensor("w1q_s", [D, DH], BF16, kind="Internal").ap()
    w2q_d = nc.dram_tensor("w2q_s", [DH, D], BF16, kind="Internal").ap()
    x1_d = nc.dram_tensor("x1_s", [B, D, T], F32, kind="Internal").ap()
    h2q_d = nc.dram_tensor("h2q_s", [B, D, T], BF16, kind="Internal").ap()
    sc1_d = nc.dram_tensor("sc1_s", [B, T], F32, kind="Internal").ap()

    dbg_d = {}
    def ddbg(name, shape, dt):
        if name in dbg:
            dbg_d[name] = nc.dram_tensor(name, shape, dt, kind="ExternalOutput").ap()
    ddbg("d_mod", [128, B * 6 * NC8], F32)
    ddbg("d_h", [B, NTA, 128, NC8, TBA], BF16)
    ddbg("d_k", [128, NC8, T], BF16)
    ddbg("d_q", [128, NC8, T], BF16)
    ddbg("d_v", [128, NTT, D], BF16)
    ddbg("d_beta", [128, NTT, NH], F32)
    ddbg("d_o", [B, 128, NC8, T], BF16)
    ddbg("d_x1", [B, D, T], F32)
    ddbg("d_h2q", [B, D, T], BF16)
    ddbg("d_h1", [B, NTD, 128, NCH, TBD + 2], BF16)
    ddbg("d_hs", [B, NTD, 128, NCH, TBD], BF16)
    ddbg("d_wq1", [D, DH], BF16)

    E = {"t": [], "v": [], "a": [], "g": [], "s": []}

    def em(eng, fn):
        E[eng].append(fn)

    def wait(eng, ctr, val):
        if val > 0:
            em(eng, lambda e, h=ctr.h, v=val: e.wait_ge(h, v))

    stack = ExitStack()

    def sem(n):
        return Ctr(stack.enter_context(nc.semaphore(n)))

    sP = sem("sP"); sV = sem("sV"); sA = sem("sA"); sG = sem("sG")
    dX = sem("dX"); dW = sem("dW"); dS = sem("dS")
    dWa = sem("dWa"); dWb = sem("dWb"); dSw = sem("dSw")
    selfsem = {e: sem("slf_" + e) for e in sync_engines} if (self_sync or race_debug) else {}
    lastmark = {}

    def guard(eng):
        for ctr in (sP, sV, sA, sG, dX, dW, dS):
            wait(eng, ctr, ctr.c)

    def guard_all():
        for eng in ("t", "v", "a", "g", "s"):
            guard(eng)

    def sb(name, shape, dt):
        return stack.enter_context(nc.sbuf_tensor(name, shape, dt))

    ones_col = sb("ones_col", [128, 1], F32)
    ident = sb("ident", [128, 128], F32)
    pair_ones = sb("pair_ones", [128, 2], BF16)
    pairsel = sb("pairsel", [2, 128], F32)
    stair = sb("stair", [128, 4, TBA], BF16)
    modv = sb("modv", [128, B * 6 * NC8], F32)
    cw_sb = sb("cw_sb", [128, 3 * NCH], F32)
    cb_sb = sb("cb_sb", [128, NCH], F32)
    beta_sb = sb("beta_sb", [128, NTT, NH], F32)
    ones_row = sb("ones_row", [1, 128], F32)
    mrow = sb("mrow", [1, TBA], F32)
    msrow = sb("msrow", [1, TBA], F32)
    invrow = sb("invrow", [1, TBA], F32)
    mirow = msrow
    ninv = sb("ninv", [2, 2, TBA], F32)
    sc_f1 = sb("sc_f1", [128, TBA], F32)
    sc_f2 = sb("sc_f2", [128, TBA], F32)
    sc_f3 = sb("sc_f3", [128, 2, TBD], F32)
    sc_f4 = sb("sc_f4", [128, TBD], F32)
    convb = sb("convb", [128, 2, TBD], F32)
    s_yrep = sb("s_yrep", [128, TBA], F32)
    mxt = s_yrep
    s_orep = sb("s_orep", [128, TBA], F32)
    sc_b1 = sb("sc_b1", [128, 2, TBD], BF16)
    sc_b2 = sb("sc_b2", [128, 2, TBD], BF16)
    dscr = sb("dscr", [128, 128], BF16)
    swc = sb("swc", [128, 4], F32)
    swn = sb("swn", [128, 2], F32)
    wsum = sb("wsum", [128, 64], F32)
    scal11 = sb("scal11", [1, 8], F32)
    scrow = sb("scrow", [1, TBD + 2], F32)
    epscol = sb("epscol", [128, 1], F32)
    vnp = sb("vnp", [1, 4], F32)
    m4 = sb("m4", [128, 4], F32)
    ksq = sc_b1[:, :, :].rearrange("p a b -> p (a b)")
    scf3f = sc_f3[:, :, :].rearrange("p a b -> p (a b)")
    convbf = convb[:, :, :].rearrange("p a b -> p (a b)")
    scb1f = sc_b1[:, :, :].rearrange("p a b -> p (a b)")
    scb2f = sc_b2[:, :, :].rearrange("p a b -> p (a b)")

    BIG = int(nc.sbuf_bytes_remaining) - 64
    assert BIG >= int(172.25 * KB), f"arena too small: {BIG}"
    arena = nc.alloc_sbuf_tensor("big_arena", [128, BIG], mybir.dt.uint8)
    abase = nc.lookup_mloc(arena).addr

    def at_(name, off_kb, shape, dt):
        return nc.alloc_sbuf_tensor_at(name, shape, dt, offset=abase + int(off_kb * KB))

    K_sb = at_("K_sb", 0, [128, NC8, T], BF16)
    Q_sb = at_("Q_sb", 32, [128, NC8, T], BF16)
    V_sb = at_("V_sb", 64, [128, NTT, D], BF16)
    h_sb = at_("h_sb", 96, [128, NC8, T], BF16)
    O_sb = at_("O_sb", 96, [128, NC8, T], BF16)
    xt_sb = at_("xt_sb", 128, [128, NC8, TBA], F32)
    wk_sb = at_("wk_sb", 128, [128, NC8, D], BF16)
    wbeta_sb = at_("wbeta_sb", 144, [128, NC8, NH], BF16)
    at_sb = at_("at_sb", 128, [128, 2, 2, TBA], BF16)
    wo_sb = at_("wo_sb", 132.25, [128, NC8, D], BF16)
    x1t_sb = at_("x1t_sb", 0, [128, NC8, TBA], F32)
    h2t_sb = at_("h2t_sb", 16, [128, NC8, TBA], F32)
    xtc_sb = at_("xtc_sb", 40, [128, NC8, TBA], F32)
    habsC = at_("habsC", 56, [128, 2, TBA], F32)
    qzC = at_("qzC", 60, [128, 2, TBA], BF16)
    qiC = at_("qiC", 62, [128, TBA], I32)
    modrows = at_("modrows", 48, [2, 6 * D], F32)
    siluc = at_("siluc", 72, [2, D], F32)
    siluct = at_("siluct", 76, [128, NC8 * 2], F32)
    onesr_bf = at_("onesr_bf", 77, [128, TBA], BF16)
    wmod_sb = at_("wmod_sb", 78, [128, 2, NC8, TBA], F32)
    w1q_sb = at_("w1q_sb", 0, [128, NC8, DH], BF16)
    h2q_res = at_("h2q_res", 64, [128, NC8, T], BF16)
    w2s_sb = at_("w2s_sb", 96, [128, 2, NCH, 128], BF16)
    h1s_sb = at_("h1s_sb", 112, [128, NCH, TBD + 2], BF16)
    hs_sb = at_("hs_sb", 128.25, [128, NCH, TBD], BF16)
    q2o_sb = at_("q2o_sb", 144.25, [128, NCH, TBD], BF16)
    outd_sb = at_("outd_sb", 160.25, [128, NC8, TBD], F32)
    qzD = at_("qzD", 168.25, [128, 2, TBA], BF16)
    qiD = at_("qiD", 170.25, [128, TBA], I32)
    wqg_s = at_("wqg_s", 148.25, [128, 4, 1024], F32)
    wqg_q = at_("wqg_q", 164.25, [128, 2, 1024], BF16)
    wqg_c = at_("wqg_c", 168.25, [128, 2, TBA], F32)
    x1t2_sb = at_("x1t2_sb", 148.25, [128, NC8, TBA], F32)
    at3_sb = at_("at3_sb", 148.25, [128, 3, 2, TBA], BF16)
    rowsQ = at_("rowsQ", 32, [1, 3, TBA], F32)

    pb = [stack.enter_context(nc.psum_tensor(f"pb{i}", [128, 512], F32)) for i in range(8)]

    def dma(eng, out, in_, sem_, **kw):
        if sem_ in (dX, dW, dS) and sem_.c > 0:
            em(eng, lambda e, h=sem_.h, v=sem_.c: e.wait_ge(h, v))
        v = sem_.inc(16)
        em(eng, lambda e, o=out, i=in_, h=sem_.h, k=kw:
           e.dma_start(out=o, in_=i, **k).then_inc(h, 16))
        return v

    def mm(out, lh, rh, start, stop, inc=None, tp=None):
        v = inc.inc() if inc is not None else None
        ih = inc.h if inc is not None else None

        def f(e, o=out, l=lh, r=rh, st=start, sp=stop, iv=v, i=ih, tpos=tp):
            ins = e.matmul(o, l, r, start=st, stop=sp, tile_position=tpos)
            if iv is not None:
                ins.then_inc(i, 1)
        em("t", f)
        return v

    def op(eng, meth, *args, inc=None, **kw):
        ss = selfsem.get(eng)
        if ss is not None and sync_mode == "chain":
            if eng in lastmark:
                mctr, mval = lastmark[eng]
                mval -= (sync_lag - 1)
                if mval > 0:
                    wait(eng, mctr, mval)
            mark = inc if inc is not None else ss
            v = inc.inc() if inc is not None else None
            if inc is None:
                ss.inc()
            lastmark[eng] = (mark, mark.c)
            ih = inc.h if inc is not None else None
            use_self = inc is None

            def f(e, m=meth, a=args, k=kw, iv=v, i=ih, sh=(ss.h if use_self else None)):
                ins = getattr(e, m)(*a, **k)
                if iv is not None:
                    ins.then_inc(i, 1)
                if sh is not None:
                    ins.then_inc(sh, 1)
            em(eng, f)
            return v
        if ss is not None and sync_mode == "flush" and inc is not None:
            sv = ss.inc()
            v = inc.inc()

            def f1(e, m=meth, a=args, k=kw, sh=ss.h):
                getattr(e, m)(*a, **k).then_inc(sh, 1)
            em(eng, f1)
            wait(eng, ss, sv)
            em(eng, lambda e, i=inc.h: e.memset(vnp[0:1, 1:2], 0.0).then_inc(i, 1))
            return v
        v = inc.inc() if inc is not None else None
        ih = inc.h if inc is not None else None

        def f(e, m=meth, a=args, k=kw, iv=v, i=ih):
            ins = getattr(e, m)(*a, **k)
            if iv is not None:
                ins.then_inc(i, 1)
        em(eng, f)
        return v

    def xv(b):
        return x_d[b].rearrange("(c p) t -> p c t", p=128)

    def x1v(b):
        return x1_d[b].rearrange("(c p) t -> p c t", p=128)

    def h2qv(b):
        return h2q_d[b].rearrange("(c p) t -> p c t", p=128)

    def outv(b):
        return out_d[b].rearrange("(c p) t -> p c t", p=128)

    def vcol(b, v6, cc):
        i = (b * 6 + v6) * NC8 + cc
        return modv[:, i:i + 1]

    def vcols(b, v6):
        i = (b * 6 + v6) * NC8
        return modv[:, i:i + NC8]

    op("g", "memset", ones_col[:], 1.0, inc=sG)
    op("g", "memset", ones_row[:], 1.0, inc=sG)
    op("g", "memset", epscol[:], EPS, inc=sG)
    op("g", "memset", onesr_bf[:], 1.0, inc=sG)
    op("g", "memset", sc_f1[:], 1.0, inc=sG)
    op("g", "memset", sc_f2[:, 0:2], 1.0, inc=sG)
    wait("g", sG, sG.c)
    op("g", "affine_select", ident[:], sc_f1[:, 0:128], [[-1, 128]], ALU.is_equal, 0.0,
       base=0, channel_multiplier=1, inc=sG)
    op("g", "affine_select", pair_ones[:, :], sc_f2[:, 0:2], [[64, 2]], ALU.is_ge, 0.0,
       base=63, channel_multiplier=-1, inc=sG)
    wait("g", sG, sG.c)
    op("g", "affine_select", pair_ones[:, :], pair_ones[:, :], [[-64, 2]], ALU.is_ge, 0.0,
       base=0, channel_multiplier=1, inc=sG)
    op("g", "affine_select", pairsel[:, :], sc_f1[0:2, 0:128], [[1, 128]], ALU.is_ge, 0.0,
       base=0, channel_multiplier=-64, inc=sG)
    wait("g", sG, sG.c)
    op("g", "affine_select", pairsel[:, :], pairsel[:, :], [[-1, 128]], ALU.is_ge, 0.0,
       base=63, channel_multiplier=64, inc=sG)
    for k in range(4):
        op("g", "affine_select", stair[:, k, :], onesr_bf[:, 0:TBA], [[-1, TBA]], ALU.is_ge,
           0.0, base=128 * k, channel_multiplier=1, inc=sG)
    v_gconst = sG.c

    dma("s", cw_sb[:], cw_d, dX)
    dma("s", cb_sb[:], cb_d, dX)
    dma("s", modrows[0:2, 0:D], c_d, dX)
    v_dx_c = dX.c

    wait("a", dX, v_dx_c)
    v_siluc = op("a", "activation", siluc[0:2, :], modrows[0:2, 0:D], AF.Silu, inc=sA)
    wait("t", sA, v_siluc)
    wait("t", sG, v_gconst)
    for cc in range(NC8):
        mm(pb[0][0:128, cc * 2:cc * 2 + 2], siluc[0:2, cc * 128:(cc + 1) * 128],
           ident[0:2, 0:2], True, True, inc=(sP if cc == NC8 - 1 else None))
    wait("v", sP, sP.c)
    v_ctcp = op("v", "tensor_copy", siluct[:], pb[0][0:128, 0:2 * NC8], inc=sV)

    NMP = 6 * D // 512
    v_modevac = [0] * NMP
    v_modmm = [0] * NMP
    wait("t", sV, v_ctcp)
    wmod_v = wmod_d.rearrange("(c p) e -> p c e", p=128)
    for j in range(NMP):
        if j >= 2:
            wait("s", sP, v_modmm[j - 2])
        vld = dma("s", wmod_sb[:, j % 2, :, :], wmod_v[:, :, j * 512:(j + 1) * 512], dX)
        wait("t", dX, vld)
        if j >= 2:
            wait("t", sV, v_modevac[j - 2])
        bank = pb[1 + (j % 2)]
        for cc in range(NC8):
            v_modmm[j] = mm(bank[0:2, :], siluct[:, cc * 2:(cc + 1) * 2],
                            wmod_sb[:, j % 2, cc, :], cc == 0, cc == NC8 - 1,
                            inc=(sP if cc == NC8 - 1 else None))
        wait("v", sP, sP.c)
        v_modevac[j] = op("v", "tensor_copy", modrows[0:2, j * 512:(j + 1) * 512],
                          bank[0:2, :], inc=sV)
    v_modrows = sV.c

    wait("t", sV, v_modrows)
    for v6 in range(6):
        for cc in range(NC8):
            i = v6 * NC8 + cc
            mm(pb[3][0:128, i * 2:i * 2 + 2],
               modrows[0:2, v6 * D + cc * 128:v6 * D + (cc + 1) * 128],
               ident[0:2, 0:2], True, True, inc=(sP if i == 47 else None))
    wait("v", sP, sP.c)
    for v6 in range(6):
        for cc in range(NC8):
            i = v6 * NC8 + cc
            for b in range(B):
                op("v", "tensor_copy", vcol(b, v6, cc), pb[3][0:128, i * 2 + b:i * 2 + b + 1])
    op("v", "memset", vnp[0:1, 0:1], 0.0, inc=sV)
    wait("v", sV, sV.c)

    for b in range(B):
        op("v", "tensor_scalar", vcols(b, 1), vcols(b, 1), 1.0, None, ALU.add)
        op("v", "tensor_scalar", vcols(b, 4), vcols(b, 4), 1.0, None, ALU.add)
        for v6 in (2, 5):
            g0 = vcols(b, v6)
            op("v", "tensor_scalar", sc_f1[:, 0:NC8], g0, 0.0, 2.0, ALU.is_ge, ALU.mult)
            op("v", "tensor_scalar", sc_f1[:, 0:NC8], sc_f1[:, 0:NC8], -1.0, None, ALU.add)
            op("v", "tensor_scalar", sc_f2[:, 0:NC8], g0, -1.0, None, ALU.mult)
            op("v", "tensor_max", sc_f2[:, 0:NC8], sc_f2[:, 0:NC8], g0)
            op("v", "tensor_scalar", sc_f2[:, 0:NC8], sc_f2[:, 0:NC8], 0.25, None, ALU.max)
            op("v", "tensor_mul", g0, sc_f1[:, 0:NC8], sc_f2[:, 0:NC8])
    v_modpost = op("v", "memset", vnp[0:1, 0:1], 0.0, inc=sV)
    if "d_mod" in dbg_d:
        wait("s", sV, v_modpost)
        dma("s", dbg_d["d_mod"], modv[:], dS)


    UN = 1024
    wq_units = []
    for ch in range(D // 128):
        for qt in range(DH // UN):
            wq_units.append((w1t_d[ch * 128:(ch + 1) * 128, qt * UN:(qt + 1) * UN],
                             w1q_d[ch * 128:(ch + 1) * 128, qt * UN:(qt + 1) * UN], 0))
    for ch in range(DH // 128):
        wq_units.append((w2t_d[ch * 128:(ch + 1) * 128, :],
                         w2q_d[ch * 128:(ch + 1) * 128, :], 1))
    NU = len(wq_units)
    assert NU == 64

    def wq_par(i):
        return dWa if i % 2 == 0 else dWb

    wq_fin = {}

    def emit_wq_finalize():
        op("v", "tensor_reduce", sc_f1[:, 0:1], wsum[:, 0:32], AX.X, ALU.add)
        v_ws = op("v", "tensor_reduce", sc_f1[:, 1:2], wsum[:, 32:64], AX.X, ALU.add,
                  inc=sV)
        wait("t", sV, v_ws)
        v_m0 = mm(pb[0][0:1, 0:2], ones_col[:, :], sc_f1[:, 0:2], True, True, inc=sP)
        wait("v", sP, v_m0)
        op("v", "tensor_scalar", scal11[0:1, 0:2], pb[0][0:1, 0:2], 1.0 / (D * DH),
           None, ALU.mult)
        op("v", "tensor_scalar", scal11[0:1, 0:2], scal11[0:1, 0:2], 1e-5, None, ALU.max)
        v_sw = op("v", "reciprocal", scal11[0:1, 2:4], scal11[0:1, 0:2], inc=sV)
        wait("t", sV, v_sw)
        v_m1 = mm(pb[1][:, 0:4], ones_row[0:1, :], scal11[0:1, 0:4], True, True, inc=sP)
        wait("v", sP, v_m1)
        op("v", "tensor_copy", swc[:, 0:1], pb[1][:, 2:3])
        op("v", "tensor_copy", swc[:, 1:2], pb[1][:, 0:1])
        op("v", "tensor_copy", swc[:, 2:3], pb[1][:, 3:4])
        op("v", "tensor_copy", swc[:, 3:4], pb[1][:, 1:2])
        op("v", "tensor_scalar", swn[:, 0:1], swc[:, 0:1], -1.0, None, ALU.mult)
        wq_fin["v"] = op("v", "tensor_scalar", swn[:, 1:2], swc[:, 2:3], -1.0, None,
                         ALU.mult, inc=sV)
        wq_fin["sem"] = sV.h

    wq1 = {"i": 0, "ld": [0] * NU, "red": [0] * NU}
    wq2 = {"i": 0, "ld": [0] * NU, "sp": [0] * NU, "tern": [0] * NU}

    def _wq_load(st, i):
        if st["ld"][i] == 0:
            if i >= 4:
                wait("s", sV, st["red"][i - 4] if "red" in st else st["tern"][i - 4])
            st["ld"][i] = dma("s", wqg_s[:, i % 4, :], wq_units[i][0], wq_par(i))

    def wq_p1(k):
        for i in range(wq1["i"], min(wq1["i"] + k, NU)):
            for pf in range(i, min(i + 4, NU)):
                _wq_load(wq1, pf)
            wait("v", wq_par(i), wq1["ld"][i])
            wq1["red"][i] = op("v", "tensor_reduce", wsum[:, i:i + 1],
                               wqg_s[:, i % 4, :], AX.X, ALU.add,
                               apply_absolute_value=True, inc=sV)
        wq1["i"] = min(wq1["i"] + k, NU)

    def wq_p2(k):
        for i in range(wq2["i"], min(wq2["i"] + k, NU)):
            for pf in range(i, min(i + 4, NU)):
                if wq2["ld"][pf] == 0:
                    if pf >= 4:
                        wait("s", sV, wq2["tern"][pf - 4])
                    else:
                        wait("s", sV, wq1["red"][NU - 4 + pf])
                    wq2["ld"][pf] = dma("s", wqg_s[:, pf % 4, :], wq_units[pf][0],
                                        wq_par(pf))
            wait("v", wq_par(i), wq2["ld"][i])
            if i >= 2:
                wait("v", dSw, wq2["sp"][i - 2])
            w12 = wq_units[i][2]
            swcol = swc[:, 0:1] if w12 == 0 else swc[:, 2:3]
            swneg = swn[:, 0:1] if w12 == 0 else swn[:, 1:2]
            for hf in range(0, UN, TBA):
                w_ = wqg_s[:, i % 4, hf:hf + TBA]
                op("v", "tensor_scalar", wqg_c[:, 0, :], w_, swcol, 0.5, ALU.mult,
                   ALU.is_ge)
                op("v", "tensor_scalar", wqg_c[:, 1, :], w_, swneg, 0.5, ALU.mult,
                   ALU.is_gt)
                op("v", "tensor_sub", wqg_q[:, i % 2, hf:hf + TBA], wqg_c[:, 0, :],
                   wqg_c[:, 1, :])
            wq2["tern"][i] = op("v", "memset", vnp[0:1, 2:3], 0.0, inc=sV)
            wait("s", sV, wq2["tern"][i])
            wq2["sp"][i] = dma("s", wq_units[i][1], wqg_q[:, i % 2, :], dSw)
        wq2["i"] = min(wq2["i"] + k, NU)


    def absmax_rows(nsub, v_src, dst_row=None):
        if dst_row is None:
            dst_row = mrow
        wait("t", sV, v_src)
        for sub in range(nsub):
            op("t", "transpose", pb[4][:, sub * 128:(sub + 1) * 128],
               mxt[:, sub * 128:(sub + 1) * 128], ident[:, :],
               inc=(sP if sub == nsub - 1 else None))
        wait("v", sP, sP.c)
        for sub in range(nsub):
            op("v", "tensor_reduce", m4[:, sub:sub + 1], pb[4][:, sub * 128:(sub + 1) * 128],
               AX.X, ALU.max, inc=(sV if sub == nsub - 1 else None))
        wait("s", sV, sV.c)
        for sub in range(nsub):
            vda = dma("s", dst_row[0:1, sub * 128:(sub + 1) * 128], m4[:, sub:sub + 1],
                      dX, allow_slow_non_contiguous=True)
        wait("v", dX, vda)
        return vda

    def ln_block(b, src_sb, dst_write, v6c, v6s, j):
        v_smm = [0] * NC8
        for cc in range(NC8):
            sqb = sc_f1 if cc % 2 == 0 else sc_f2
            if cc >= 2:
                wait("a", sP, v_smm[cc - 2])
            va = op("a", "activation", sqb[:], src_sb[:, cc, :], AF.Square, inc=sA)
            wait("t", sA, va)
            mm(pb[2][0:1, :], ones_col[:, :], src_sb[:, cc, :], cc == 0, cc == NC8 - 1)
            v_smm[cc] = mm(pb[3][0:1, :], ones_col[:, :], sqb[:], cc == 0, cc == NC8 - 1,
                           inc=sP)
        wait("v", sP, sP.c)
        op("v", "tensor_scalar", mrow[0:1, :], pb[2][0:1, :], 1.0 / D, None, ALU.mult)
        op("v", "tensor_scalar", msrow[0:1, :], pb[3][0:1, :], 1.0 / D, None, ALU.mult)
        op("v", "tensor_mul", invrow[0:1, :], mrow[0:1, :], mrow[0:1, :])
        v_var = op("v", "tensor_sub", invrow[0:1, :], msrow[0:1, :], invrow[0:1, :], inc=sV)
        wait("a", sV, v_var)
        v_ln = op("a", "activation", msrow[0:1, :], invrow[0:1, :], AF.Ln, bias=epscol[0:1, 0:1], inc=sA)
        wait("a", sA, v_ln)
        v_iv = op("a", "activation", invrow[0:1, :], msrow[0:1, :], AF.Exp, scale=-0.5,
                  inc=sA)
        wait("v", sA, v_iv)
        v_mi = op("v", "scalar_tensor_tensor", mirow[0:1, :], mrow[0:1, :], -1.0,
                  invrow[0:1, :], ALU.mult, ALU.mult, inc=sV)
        wait("t", sV, v_mi)
        pI = pb[6]
        pM = pb[7]
        mm(pI[:, :], ones_row[0:1, :], invrow[0:1, :], True, True)
        mm(pM[:, :], ones_row[0:1, :], mirow[0:1, :], True, True, inc=sP)
        wait("v", sP, sP.c)
        v_hc = 0
        for cc in range(NC8):
            op("v", "tensor_tensor", sc_f2[:], src_sb[:, cc, :], pI[:, :], ALU.mult)
            op("v", "tensor_tensor", sc_f2[:], sc_f2[:], pM[:, :], ALU.add)
            v_hc = op("v", "tensor_scalar", dst_write(cc), sc_f2[:], vcol(b, v6c, cc),
                      vcol(b, v6s, cc), ALU.mult, ALU.add, inc=sV)
        return v_hc

    def phase_A(b):
        guard_all()
        for j in range(NTA):
            t0 = j * TBA
            vld = dma("s", xt_sb[:], xv(b)[:, :, t0:t0 + TBA], dX)
            wait("a", dX, vld)
            wait("t", dX, vld)
            v_h = ln_block(b, xt_sb,
                           lambda cc, t0=t0: h_sb[:, cc, t0:t0 + TBA], 1, 0, j)
            if "d_h" in dbg_d:
                wait("s", sV, v_h)
                dma("s", dbg_d["d_h"][b, j], h_sb[:, :, t0:t0 + TBA], dS)
            wait("s", sV, v_h)

        ksqp = (sc_b1[:, :, :].rearrange("p a b -> p (a b)"),
                sc_b2[:, :, :].rearrange("p a b -> p (a b)"))
        for si, (Wd, Out, isq) in enumerate(((wk_d, K_sb, 0), (wq_d, Q_sb, 1))):
            if b == 0 and si == 1:
                emit_wq_finalize()
            wait("g", sP, sP.c)
            wait("g", sV, sV.c)
            v_w = dma("g", wk_sb[:], Wd.rearrange("(c p) e -> p c e", p=128), dW)
            wait("t", dW, v_w)
            jobs = [(j, ec) for j in range(NTA) for ec in range(NC8)]
            n = len(jobs)
            v_proj = [0] * n; v_cp = [0] * n; v_sq = [0] * n; v_nm = [0] * n
            v_l2e = [0] * n; v_nr = [0] * n; v_rep = [0] * n; v_fin = [0] * n

            def t_proj(u):
                j, ec = jobs[u]
                pbank = pb[3] if u % 2 == 0 else pb[2]
                if u >= 2:
                    wait("t", sA, v_sq[u - 2])
                for cc in range(NC8):
                    mm(pbank[:, :], wk_sb[:, cc, ec * 128:(ec + 1) * 128],
                       h_sb[:, cc, jobs[u][0] * TBA:jobs[u][0] * TBA + TBA],
                       cc == 0, cc == NC8 - 1, inc=(sP if cc == NC8 - 1 else None))
                v_proj[u] = sP.c

            def a_copy_sq(u):
                j, ec = jobs[u]
                t0 = j * TBA
                pbank = pb[3] if u % 2 == 0 else pb[2]
                wait("a", sP, v_proj[u])
                if u >= 2:
                    wait("a", sP, v_nm[u - 2])
                v_cp[u] = op("a", "activation", Out[:, ec, t0:t0 + TBA], pbank[:, :],
                             AF.Identity, inc=sA)
                v_sq[u] = op("a", "activation", ksqp[u % 2], Out[:, ec, t0:t0 + TBA],
                             AF.Square, inc=sA)

            def t_norm(u):
                nbank = pb[5] if u % 2 == 0 else pb[4]
                wait("t", sA, v_sq[u])
                if u >= 2:
                    wait("t", sV, v_fin[u - 2])
                v_nm[u] = mm(nbank[0:2, :], pair_ones[:, :], ksqp[u % 2], True, True,
                             inc=sP)

            def a_lnexp(u):
                nbank = pb[5] if u % 2 == 0 else pb[4]
                wait("a", sP, v_nm[u])
                if u >= 2:
                    wait("a", sP, v_rep[u - 2])
                op("a", "activation", ninv[0:2, u % 2, :], nbank[0:2, :], AF.Ln)
                v_l2e[u] = op("a", "activation", ninv[0:2, u % 2, :],
                              ninv[0:2, u % 2, :], AF.Exp, scale=-0.5, inc=sA)

            def v_minc(u):
                wait("v", sA, v_l2e[u])
                op("v", "tensor_scalar", ninv[0:2, u % 2, :], ninv[0:2, u % 2, :], 1e12,
                   None, ALU.min)
                if isq:
                    op("v", "tensor_scalar", ninv[0:2, u % 2, :], ninv[0:2, u % 2, :],
                       0.125, None, ALU.mult)
                v_nr[u] = op("v", "memset", vnp[0:1, 0:1], 0.0, inc=sV)

            def t_rep(u):
                nbank = pb[5] if u % 2 == 0 else pb[4]
                wait("t", sV, v_nr[u])
                wait("t", sA, v_l2e[u])
                v_rep[u] = mm(nbank[:, :], pairsel[:, :], ninv[0:2, u % 2, :], True, True,
                              inc=sP)

            def v_fin_(u):
                j, ec = jobs[u]
                t0 = j * TBA
                nbank = pb[5] if u % 2 == 0 else pb[4]
                wait("v", sP, v_rep[u])
                wait("v", sA, v_cp[u])
                v_fin[u] = op("v", "tensor_tensor", Out[:, ec, t0:t0 + TBA],
                              Out[:, ec, t0:t0 + TBA], nbank[:, :], ALU.mult, inc=sV)

            for u in range(n):
                t_proj(u)
                a_copy_sq(u)
                if u >= 1:
                    t_norm(u - 1)
                    a_lnexp(u - 1)
                    v_minc(u - 1)
                if u >= 2:
                    t_rep(u - 2)
                    v_fin_(u - 2)
                if b == 0:
                    wq_p1(2) if si == 0 else wq_p2(1)
            t_norm(n - 1)
            a_lnexp(n - 1)
            v_minc(n - 1)
            t_rep(n - 2)
            v_fin_(n - 2)
            t_rep(n - 1)
            v_fin_(n - 1)

        wait("g", sP, sP.c)
        wait("g", sV, sV.c)
        v_w = dma("g", wk_sb[:], wv_d.rearrange("(c p) e -> p c e", p=128), dW)
        wait("t", dW, v_w)
        for tt in range(NTT):
            tj, tt4 = divmod(tt, TBA // 128)
            t0 = tj * TBA
            for half in range(2):
                wait("t", sA, sA.c)
                for cc in range(NC8):
                    mm(pb[4][:, :], h_sb[:, cc, tt * 128:(tt + 1) * 128],
                       wk_sb[:, cc, half * 512:(half + 1) * 512], cc == 0, cc == NC8 - 1,
                       inc=(sP if cc == NC8 - 1 else None))
                wait("a", sP, sP.c)
                op("a", "activation", V_sb[:, tt, half * 512:(half + 1) * 512],
                   pb[4][:, :], AF.Identity, inc=sA)
            if b == 0:
                wq_p2(1)
        wait("g", sA, sA.c)
        v_w = dma("g", wbeta_sb[:], wb_d.rearrange("(c p) e -> p c e", p=128), dW)
        wait("t", dW, v_w)
        for tt in range(NTT):
            wait("t", sA, sA.c)
            for cc in range(NC8):
                mm(pb[2][:, 0:NH], h_sb[:, cc, tt * 128:(tt + 1) * 128],
                   wbeta_sb[:, cc, :], cc == 0, cc == NC8 - 1,
                   inc=(sP if cc == NC8 - 1 else None))
            wait("a", sP, sP.c)
            v_be = op("a", "activation", beta_sb[:, tt, :], pb[2][:, 0:NH], AF.Exp,
                      scale=-1.0, inc=sA)
            wait("v", sA, v_be)
            op("v", "tensor_scalar", beta_sb[:, tt, :], beta_sb[:, tt, :], 1.0, None,
               ALU.add)
            op("v", "reciprocal", beta_sb[:, tt, :], beta_sb[:, tt, :], inc=sV)
            if b == 0:
                wq_p2(1)
        if "d_k" in dbg_d:
            wait("s", sV, sV.c)
            dma("s", dbg_d["d_k"], K_sb[:], dS)
            dma("s", dbg_d["d_q"], Q_sb[:], dS)
        if "d_v" in dbg_d:
            wait("s", sA, sA.c)
            wait("s", sV, sV.c)
            dma("s", dbg_d["d_v"], V_sb[:], dS)
            dma("s", dbg_d["d_beta"], beta_sb[:], dS)

    def phase_B(b):
        if b == 0:
            wq_p1(NU)
            wq_p2(NU)
        guard_all()
        v_wo = dma("g", wo_sb[:], wo_d.rearrange("(c p) e -> p c e", p=128), dW)
        m_oev = [0]
        for p in range(NC8):
            e0 = p * 128
            for j in range(NTA):
                t0 = j * TBA
                wait("t", sA, m_oev[0])
                ks = list(range(4 * j, NTT))
                v_at_v = [0] * len(ks)
                v_at_a = [0] * len(ks)
                v_omm = [0] * len(ks)

                sbanks = ((pb[4], pb[5]), (pb[6], pb[7]), (pb[2], pb[3]))

                def emit_scores(k):
                    i = ks[k]
                    par = k % 3
                    if k >= 3:
                        wait("t", sV, v_at_v[k - 3])
                        wait("t", sA, v_at_a[k - 3])
                    mm(sbanks[par][0][:, :], K_sb[0:64, p, i * 128:i * 128 + 128],
                       Q_sb[0:64, p, t0:t0 + TBA], True, True, tp=(0, 0), inc=sP)
                    mm(sbanks[par][1][:, :], K_sb[64:128, p, i * 128:i * 128 + 128],
                       Q_sb[64:128, p, t0:t0 + TBA], True, True, tp=(64, 0), inc=sP)
                    return sP.c

                def emit_evac(k, v_sc):
                    i = ks[k]
                    par = k % 3
                    diag = i - 4 * j if (i - 4 * j) < 4 else None
                    banks = sbanks[par]
                    if diag is not None:
                        c0, c1 = diag * 128, (diag + 1) * 128
                        wait("v", sP, v_sc)
                        for hp, bank in ((0, banks[0]), (1, banks[1])):
                            op("v", "scalar_tensor_tensor", at3_sb[:, par, hp, :],
                               bank[:, :], 1.0, stair[:, diag, :],
                               ALU.mult, ALU.mult)
                            bcol = beta_sb[:, i, 2 * p + hp:2 * p + hp + 1]
                            op("v", "tensor_scalar", dscr[:, :], ident[:, :], bcol, None,
                               ALU.mult)
                            op("v", "tensor_add", at3_sb[:, par, hp, c0:c1],
                               at3_sb[:, par, hp, c0:c1], dscr[:, :], inc=sV)
                        v_at_v[k] = sV.c
                        v_at_a[k] = 0
                    else:
                        wait("a", sP, v_sc)
                        v_at_a[k] = op("a", "activation", at3_sb[:, par, 0, :],
                                       banks[0][:, :], AF.Identity, inc=sA)
                        wait("v", sP, v_sc)
                        v_at_v[k] = op("v", "tensor_copy", at3_sb[:, par, 1, :],
                                       banks[1][:, :], inc=sV)

                def emit_O(k):
                    i = ks[k]
                    par = k % 3
                    first = (k == 0)
                    last = (k == len(ks) - 1)
                    if v_at_v[k]:
                        wait("t", sV, v_at_v[k])
                    if v_at_a[k]:
                        wait("t", sA, v_at_a[k])
                    mm(pb[0][0:64, :], V_sb[:, i, e0:e0 + 64], at3_sb[:, par, 0, :],
                       first, False, tp=(0, 0))
                    v_omm[k] = mm(pb[1][64:128, :], V_sb[:, i, e0 + 64:e0 + 128],
                                  at3_sb[:, par, 1, :], first, last, tp=(0, 64), inc=sP)

                for k in range(len(ks)):
                    v_sc = emit_scores(k)
                    emit_evac(k, v_sc)
                    if k >= 1:
                        emit_O(k - 1)
                emit_O(len(ks) - 1)
                wait("a", sP, sP.c)
                op("a", "activation", O_sb[0:64, p, t0:t0 + TBA], pb[0][0:64, :],
                   AF.Identity)
                m_oev[0] = op("a", "activation", O_sb[64:128, p, t0:t0 + TBA],
                              pb[1][64:128, :], AF.Identity, inc=sA)
        if "d_o" in dbg_d:
            wait("s", sA, sA.c)
            dma("s", dbg_d["d_o"][b], O_sb[:], dS)
        return dW.c

    def phase_C(b, v_wo):
        guard_all()
        wait("t", dW, v_wo)
        x1tp = (x1t_sb, x1t2_sb)
        mrowQ = rowsQ[0:1, 0, :]
        invQ = rowsQ[0:1, 1, :]
        miQ = rowsQ[0:1, 2, :]
        ypar1 = (scf3f, convbf)
        MC = [dict() for _ in range(NTA)]

        def ld_x(j):
            if j >= 1:
                wait("s", sV, MC[j - 1]["x1last"])
            MC[j]["ldx"] = dma("s", xtc_sb[:], xv(b)[:, :, j * TBA:(j + 1) * TBA], dX)

        def lnC_stats(j):
            src = x1tp[j % 2]
            v_smm = [0] * NC8
            for cc in range(NC8):
                sqb = sc_f1 if cc % 2 == 0 else sc_f2
                if cc == 0:
                    wait("a", sV, MC[j]["x1last"])
                if cc >= 2:
                    wait("a", sP, v_smm[cc - 2])
                va = op("a", "activation", sqb[:], src[:, cc, :], AF.Square, inc=sA)
                wait("t", sA, va)
                mm(pb[2][0:1, :], ones_col[:, :], src[:, cc, :], cc == 0, cc == NC8 - 1)
                v_smm[cc] = mm(pb[3][0:1, :], ones_col[:, :], sqb[:], cc == 0,
                               cc == NC8 - 1, inc=sP)
            wait("v", sP, sP.c)
            op("v", "tensor_scalar", mrow[0:1, :], pb[2][0:1, :], 1.0 / D, None,
               ALU.mult)
            op("v", "tensor_scalar", msrow[0:1, :], pb[3][0:1, :], 1.0 / D, None,
               ALU.mult)
            op("v", "tensor_mul", invrow[0:1, :], mrow[0:1, :], mrow[0:1, :])
            v_var = op("v", "tensor_sub", invrow[0:1, :], msrow[0:1, :],
                       invrow[0:1, :], inc=sV)
            wait("a", sV, v_var)
            v_ln = op("a", "activation", msrow[0:1, :], invrow[0:1, :], AF.Ln,
                      bias=epscol[0:1, 0:1], inc=sA)
            wait("a", sA, v_ln)
            v_iv = op("a", "activation", invrow[0:1, :], msrow[0:1, :], AF.Exp,
                      scale=-0.5, inc=sA)
            wait("v", sA, v_iv)
            v_mi = op("v", "scalar_tensor_tensor", msrow[0:1, :], mrow[0:1, :], -1.0,
                      invrow[0:1, :], ALU.mult, ALU.mult, inc=sV)
            wait("t", sV, v_mi)
            mm(pb[6][:, :], ones_row[0:1, :], invrow[0:1, :], True, True)
            v_rep = mm(pb[7][:, :], ones_row[0:1, :], msrow[0:1, :], True, True,
                       inc=sP)
            MC[j]["lnrep"] = v_rep

        def tail_pieces(j):
            t0 = j * TBA
            st = {"zk": [0] * NC8, "maxm": [0] * NC8, "hlast": 0}

            def hprod(cc):
                if cc == 0:
                    wait("v", sP, MC[j]["lnrep"])
                op("v", "tensor_tensor", habsC[:, 0, :], x1tp[j % 2][:, cc, :],
                   pb[6][:, :], ALU.mult)
                op("v", "tensor_tensor", habsC[:, 0, :], habsC[:, 0, :], pb[7][:, :],
                   ALU.add)
                st["hlast"] = op("v", "tensor_scalar", h2t_sb[:, cc, :],
                                 habsC[:, 0, :], vcol(b, 4, cc), vcol(b, 3, cc),
                                 ALU.mult, ALU.add, inc=sV)

            def absmax_piece():
                op("v", "memset", mxt[:], 0.0)
                v_absc = [0] * NC8
                for cc in range(NC8):
                    if cc == 0:
                        wait("a", sV, st["hlast"])
                    if cc >= 2:
                        wait("a", sV, st["maxm"][cc - 2])
                    v_absc[cc] = op("a", "activation", habsC[:, cc % 2, :],
                                    h2t_sb[:, cc, :], AF.Abs, inc=sA)
                    wait("v", sA, v_absc[cc])
                    st["maxm"][cc] = op("v", "tensor_max", mxt[:], mxt[:],
                                        habsC[:, cc % 2, :], inc=sV)
                absmax_rows(TBA // 128, sV.c, mrowQ)
                op("v", "tensor_scalar", mrowQ[0:1, :], mrowQ[0:1, :], 1e-5, None,
                   ALU.max)
                op("v", "reciprocal", invQ[0:1, :], mrowQ[0:1, :])
                op("v", "tensor_scalar", invQ[0:1, :], invQ[0:1, :], 127.0, None,
                   ALU.mult)
                op("v", "reciprocal", miQ[0:1, :], invQ[0:1, :])
                v_row = op("v", "tensor_scalar", miQ[0:1, :], miQ[0:1, :],
                           scal11[0:1, 0:1], None, ALU.mult, inc=sV)
                wait("t", sV, v_row)
                mm(pb[5][:, :], ones_row[0:1, :], invQ[0:1, :], True, True, inc=sP)
                wait("v", sP, sP.c)
                op("v", "tensor_copy", s_yrep[:], pb[5][:, :], inc=sV)

            def q1_y(k):
                return op("v", "tensor_mul", ypar1[k % 2],
                          h2t_sb[:, k, :], s_yrep[:, :], inc=sV)

            def q1_cast(k, v_y):
                wait("a", sV, v_y)
                op("a", "activation", qiC[:], ypar1[k % 2], AF.Identity)
                return op("a", "activation", qzC[:, k % 2, :], qiC[:], AF.Identity,
                          inc=sA)

            def q1_ins(k, v_zf):
                wait("v", sA, v_zf)
                op("v", "tensor_sub", ypar1[k % 2], ypar1[k % 2], qzC[:, k % 2, :])
                op("v", "scalar_tensor_tensor", scb2f, ypar1[k % 2], -0.5,
                   qzC[:, k % 2, :], ALU.is_lt, ALU.subtract)
                return op("v", "scalar_tensor_tensor", h2q_res[:, k, t0:t0 + TBA],
                          ypar1[k % 2], 0.5, scb2f, ALU.is_gt, ALU.subtract, inc=sV)

            def quant_piece(ks, last):
                for k in ks:
                    st["zk"][k] = q1_cast(k, q1_y(k))
                    if k >= 1:
                        q1_ins(k - 1, st["zk"][k - 1])
                if last:
                    q1_ins(NC8 - 1, st["zk"][NC8 - 1])
                    wait("s", sV, sV.c)
                    dma("s", sc1_d[b, t0:t0 + TBA].unsqueeze(0), miQ[0:1, :], dS)

            return [lambda: (hprod(0), hprod(1)),
                    lambda: (hprod(2), hprod(3)),
                    lambda: (hprod(4), hprod(5)),
                    lambda: (hprod(6), hprod(7)),
                    lambda: absmax_piece(),
                    lambda: quant_piece((0, 1, 2), False),
                    lambda: quant_piece((3, 4, 5), False),
                    lambda: quant_piece((6, 7), True)]

        def emit_wout(j, pieces):
            t0 = j * TBA
            xt = x1tp[j % 2]
            v_x1c = [0] * NC8
            for dc in range(NC8):
                pbank = pb[dc % 2]
                if dc == 0 and j >= 2:
                    wait("v", dS, MC[j - 2]["spill"])
                if dc >= 2:
                    wait("t", sV, v_x1c[dc - 2])
                for ec in range(NC8):
                    mm(pbank[:, :], wo_sb[:, ec, dc * 128:(dc + 1) * 128],
                       O_sb[:, ec, t0:t0 + TBA], ec == 0, ec == NC8 - 1,
                       inc=(sP if ec == NC8 - 1 else None))
                wait("v", sP, sP.c)
                if dc == 0:
                    wait("v", dX, MC[j]["ldx"])
                v_x1c[dc] = op("v", "scalar_tensor_tensor", xt[:, dc, :], pbank[:, :],
                               vcol(b, 2, dc), xtc_sb[:, dc, :], ALU.mult, ALU.add,
                               inc=sV)
                if pieces is not None:
                    pieces[dc]()
            MC[j]["x1last"] = v_x1c[NC8 - 1]
            wait("s", sV, v_x1c[NC8 - 1])
            MC[j]["spill"] = dma("s", x1v(b)[:, :, t0:t0 + TBA], xt[:], dS)
            if j + 1 < NTA:
                ld_x(j + 1)

        ld_x(0)
        emit_wout(0, None)
        for j in range(1, NTA):
            lnC_stats(j - 1)
            emit_wout(j, tail_pieces(j - 1))
        lnC_stats(NTA - 1)
        for pc in tail_pieces(NTA - 1):
            pc()

    def phase_D(b):
        guard_all()
        wait("s", dSw, dSw.c)
        w1v = w1q_d.rearrange("(c p) e -> p c e", p=128)
        vw1a = dma("s", w1q_sb[:, 0:NC8 // 2, :], w1v[:, 0:NC8 // 2, :], dWa)
        vw1 = dma("s", w1q_sb[:, NC8 // 2:NC8, :], w1v[:, NC8 // 2:NC8, :], dW)
        w2v = w2q_d.rearrange("(c p) e -> p c e", p=128)

        NK = NCH // 2
        x1dv = sc_f3
        osclv = sc_f2[:, 0:TBD]
        habs = (sc_f1[:, 0:TBD], sc_f1[:, TBD:2 * TBD])
        ypar2 = (scf3f, convbf)
        syb = s_yrep[:, 0:TBD].unsqueeze(1).broadcast_to([128, 2, TBD])
        M = [dict() for _ in range(NTD)]

        def srow(j):
            return ninv[0:1, j % 2, 0:TBD + 2]

        def scrow_load(j):
            t0 = j * TBD
            lo = max(t0 - 1, 0)
            hi = min(t0 + TBD + 1, T)
            padL = 1 if j == 0 else 0
            if j >= 2:
                wait("s", sP, M[j - 2]["toprep"])
            M[j]["scld"] = dma("s", srow(j)[0:1, padL:padL + (hi - lo)],
                               sc1_d[b, lo:hi].unsqueeze(0), dX)

        def w2s01_load(j):
            if j >= 1:
                wait("s", sP, M[j - 1]["w2mm"])
            M[j]["w2s01"] = ((dW, dma("s", w2s_sb[:, 0, :, :], w2v[:, :, 0:128], dW)),
                             (dWb, dma("s", w2s_sb[:, 1, :, :], w2v[:, :, 128:256], dWb)))

        def emit_top(j):
            wait("v", dX, M[j]["scld"])
            if j == 0:
                op("v", "memset", srow(j)[0:1, 0:1], 0.0)
            if j == NTD - 1:
                op("v", "memset", srow(j)[0:1, TBD + 1:TBD + 2], 0.0)
            if j == 0 or j == NTD - 1:
                ecol = 0 if j == 0 else TBD + 1
                op("v", "memset", h1s_sb[:, :, ecol:ecol + 1], 0.0)
            v_e = op("v", "memset", vnp[0:1, 0:1], 0.0, inc=sV)
            M[j]["edge"] = v_e
            wait("t", sV, v_e)
            M[j]["toprep"] = mm(pb[7][:, 0:TBD + 2], ones_row[0:1, :], srow(j),
                                True, True, inc=sP)
            wait("v", sP, sP.c)
            op("v", "tensor_copy", s_orep[:, 0:TBD + 2], pb[7][:, 0:TBD + 2], inc=sV)

        def q2_y(k):
            return op("v", "tensor_mul",
                      ypar2[k % 2].rearrange("p (a b) -> p a b", a=2),
                      hs_sb[:, 2 * k:2 * k + 2, :], syb, inc=sV)

        def q2_cast(k, v_y):
            wait("a", sV, v_y)
            op("a", "activation", qiD[:], ypar2[k % 2], AF.Identity)
            return op("a", "activation", qzD[:, k % 2, :], qiD[:], AF.Identity,
                      inc=sA)

        def q2_ins(k, v_zf):
            wait("v", sA, v_zf)
            op("v", "tensor_sub", ypar2[k % 2], ypar2[k % 2], qzD[:, k % 2, :])
            op("v", "scalar_tensor_tensor", scb2f, ypar2[k % 2], -0.5,
               qzD[:, k % 2, :], ALU.is_lt, ALU.subtract)
            return op("v", "scalar_tensor_tensor",
                      q2o_sb[:, 2 * k:2 * k + 2, 0:TBD],
                      ypar2[k % 2].rearrange("p (a b) -> p a b", a=2), 0.5,
                      scb2f.rearrange("p (a b) -> p a b", a=2),
                      ALU.is_gt, ALU.subtract, inc=sV)

        def emit_H1(j, qj):
            t0 = j * TBD
            lo = max(t0 - 1, 0)
            hi = min(t0 + TBD + 1, T)
            pL = 1 if j == 0 else 0
            pR = 1 if j == NTD - 1 else 0
            W = TBD + 2 - pL - pR
            v_ec = [0] * NCH
            vy = [0] * NK
            vz = [0] * NK
            for cc in range(NCH):
                pbank = pb[(0, 1, 5, 6)[cc % 4]]
                if cc == 0:
                    if j == 0:
                        wait("t", dW, vw1)
                        wait("t", dWa, vw1a)
                    else:
                        wait("t", sV, M[j - 1]["convv"])
                        wait("t", sA, M[j - 1]["tapa"])
                    wait("t", sV, M[j]["edge"])
                if cc >= 4:
                    wait("t", sV, v_ec[cc - 4])
                for dc in range(NC8):
                    mm(pbank[:, pL:pL + W], w1q_sb[:, dc, cc * 128:(cc + 1) * 128],
                       h2q_res[:, dc, lo:hi], dc == 0, dc == NC8 - 1,
                       inc=(sP if dc == NC8 - 1 else None))
                wait("v", sP, sP.c)
                v_ec[cc] = op("v", "scalar_tensor_tensor", h1s_sb[:, cc, pL:pL + W],
                              pbank[:, pL:pL + W], 1.0, s_orep[:, pL:pL + W],
                              ALU.mult, ALU.mult, inc=sV)
                if qj is not None and cc % 2 == 1:
                    k = cc // 2
                    vy[k] = q2_y(k)
                    vz[k] = q2_cast(k, vy[k])
                    if k >= 1:
                        q2_ins(k - 1, vz[k - 1])
            if qj is not None:
                q2_ins(NK - 1, vz[NK - 1])
                M[qj]["q2"] = sV.c
            M[j]["vec"] = v_ec

        def emit_quant_tail(qj):
            vy = [0] * NK
            vz = [0] * NK
            for k in range(NK):
                vy[k] = q2_y(k)
                vz[k] = q2_cast(k, vy[k])
                if k >= 1:
                    q2_ins(k - 1, vz[k - 1])
            q2_ins(NK - 1, vz[NK - 1])
            M[qj]["q2"] = sV.c

        def emit_W2conv(jw, cj):
            if cj is not None:
                vec = M[cj]["vec"]
                v_tap = [0] * NCH
                v_cvk = [0] * NCH
                v_abs = [0] * NCH
                v_max = [0] * NCH

                def tap0(cc):
                    if cc == 0:
                        if cj >= 1:
                            wait("a", sV, M[cj - 1]["q2"])
                    wait("a", sV, vec[cc])
                    v_tap[cc] = op("a", "activation", convb[:, cc % 2, :],
                                   h1s_sb[:, cc, 0:TBD], AF.Identity,
                                   scale=cw_sb[:, cc * 3:cc * 3 + 1], inc=sA)

                def conv_step(cc):
                    if cc == 0:
                        tap0(0)
                        op("v", "memset", mxt[:], 0.0)
                    if cc + 1 < NCH:
                        tap0(cc + 1)
                    cb_ = convb[:, cc % 2, :]
                    wait("v", sA, v_tap[cc])
                    op("v", "scalar_tensor_tensor", cb_, h1s_sb[:, cc, 1:TBD + 1],
                       cw_sb[:, cc * 3 + 1:cc * 3 + 2], cb_, ALU.mult, ALU.add)
                    v_cvk[cc] = op("v", "scalar_tensor_tensor", cb_,
                                   h1s_sb[:, cc, 2:TBD + 2],
                                   cw_sb[:, cc * 3 + 2:cc * 3 + 3], cb_,
                                   ALU.mult, ALU.add, inc=sV)
                    wait("a", sV, v_cvk[cc])
                    op("a", "activation", hs_sb[:, cc, :], convb[:, cc % 2, :],
                       AF.Silu, bias=cb_sb[:, cc:cc + 1])
                    if cc >= 2:
                        wait("a", sV, v_max[cc - 2])
                    v_abs[cc] = op("a", "activation", habs[cc % 2],
                                   hs_sb[:, cc, :], AF.Abs, inc=sA)
                    wait("v", sA, v_abs[cc])
                    v_max[cc] = op("v", "tensor_max", mxt[:, (cc % 2) * TBD:
                                   (cc % 2) * TBD + TBD],
                                   mxt[:, (cc % 2) * TBD:(cc % 2) * TBD + TBD],
                                   habs[cc % 2], inc=sV)

            if jw is not None:
                t0w = jw * TBD
                vws = [0] * NC8
                vws[0], vws[1] = M[jw]["w2s01"]
                v_oc = [0] * NC8
            for dc in range(NC8):
                if jw is not None:
                    if dc == 0:
                        wait("s", sV, M[jw]["q2"])
                    if dc >= 2:
                        wait("s", sV, v_oc[dc - 2])
                    vxs = dma("s", x1dv[:, dc % 2, :],
                              x1v(b)[:, dc, t0w:t0w + TBD], dX)
                    pbank = pb[2 + dc % 2]
                    if dc == 0:
                        wait("t", sV, M[jw]["q2"])
                        if jw >= 1:
                            wait("v", dS, M[jw - 1]["store"])
                    if dc >= 2:
                        wait("t", sV, v_oc[dc - 2])
                    wait("t", vws[dc][0], vws[dc][1])
                    for cc in range(NCH):
                        vmm = mm(pbank[:, 0:TBD], w2s_sb[:, dc % 2, cc, :],
                                 q2o_sb[:, cc, 0:TBD], cc == 0, cc == NCH - 1,
                                 inc=(sP if cc == NCH - 1 else None))
                    if dc + 2 < NC8:
                        wait("s", sP, sP.c)
                        q_ = dW if dc % 2 == 0 else dWb
                        vws[dc + 2] = (q_, dma("s", w2s_sb[:, dc % 2, :, :],
                                          w2v[:, :, (dc + 2) * 128:(dc + 3) * 128], q_))
                    else:
                        M[jw]["w2mm"] = vmm
                    wait("v", sP, sP.c)
                    wait("v", dX, vxs)
                    op("v", "scalar_tensor_tensor", sc_f4[:], pbank[:, 0:TBD], 1.0,
                       osclv, ALU.mult, ALU.mult)
                    v_oc[dc] = op("v", "scalar_tensor_tensor", outd_sb[:, dc, :],
                                  sc_f4[:], vcol(b, 5, dc), x1dv[:, dc % 2, :],
                                  ALU.mult, ALU.add, inc=sV)
                    if dc == 3 and jw + 2 < NTD:
                        scrow_load(jw + 2)
                if cj is not None:
                    for q in range(4):
                        conv_step(dc * 4 + q)
            if cj is not None:
                M[cj]["convv"] = v_cvk[NCH - 1]
                M[cj]["tapa"] = v_tap[NCH - 1]
            if jw is not None:
                if jw + 1 < NTD:
                    w2s01_load(jw + 1)
                wait("s", sV, v_oc[NC8 - 1])
                M[jw]["store"] = dma("s", outv(b)[:, :, t0w:t0w + TBD], outd_sb[:], dS)

        def emit_absfin(j):
            v_hs = op("v", "tensor_max", mxt[:, 0:TBD], mxt[:, 0:TBD],
                      mxt[:, TBD:2 * TBD], inc=sV)
            absmax_rows(TBD // 128, v_hs)
            op("v", "tensor_scalar", mrow[0:1, 0:TBD], mrow[0:1, 0:TBD], 1e-5, None,
               ALU.max)
            op("v", "reciprocal", invrow[0:1, 0:TBD], mrow[0:1, 0:TBD])
            op("v", "tensor_scalar", invrow[0:1, 0:TBD], invrow[0:1, 0:TBD], 127.0,
               None, ALU.mult)
            op("v", "reciprocal", mirow[0:1, 0:TBD], invrow[0:1, 0:TBD])
            v_row = op("v", "tensor_scalar", mirow[0:1, 0:TBD], mirow[0:1, 0:TBD],
                       scal11[0:1, 1:2], None, ALU.mult, inc=sV)
            wait("t", sV, v_row)
            mm(pb[5][:, 0:TBD], ones_row[0:1, :], invrow[0:1, 0:TBD], True, True)
            mm(pb[6][:, 0:TBD], ones_row[0:1, :], mirow[0:1, 0:TBD], True, True,
               inc=sP)
            wait("v", sP, sP.c)
            op("v", "tensor_copy", s_yrep[:, 0:TBD], pb[5][:, 0:TBD])
            op("v", "tensor_copy", osclv, pb[6][:, 0:TBD], inc=sV)

        scrow_load(0)
        scrow_load(1)
        w2s01_load(0)
        emit_top(0)
        emit_H1(0, None)
        emit_W2conv(None, 0)
        emit_absfin(0)
        for j in range(1, NTD):
            emit_top(j)
            emit_H1(j, j - 1)
            emit_W2conv(j - 1, j)
            emit_absfin(j)
        emit_quant_tail(NTD - 1)
        emit_W2conv(NTD - 1, None)

    for b in range(B):
        phase_A(b)
        v_wo = phase_B(b)
        phase_C(b, v_wo)
        phase_D(b)

    em("s", lambda e, h=dS.h: e.wait_ge(h, dS.c))
    em("g", lambda e, h=dS.h: e.wait_ge(h, dS.c))

    with nc.Block() as block:
        @block.tensor
        def _(t):
            for f in E["t"]:
                f(t)

        @block.vector
        def _(v):
            for f in E["v"]:
                f(v)

        @block.scalar
        def _(a):
            for f in E["a"]:
                f(a)

        @block.gpsimd
        def _(g):
            for f in E["g"]:
                f(g)

        @block.sync
        def _(s):
            for f in E["s"]:
                f(s)

    return nc


def _shards(L0_attn_x, L0_attn_Wk, L0_attn_Wv, L0_attn_Wq, L0_attn_Wbeta, L0_attn_Wout,
            L0_ffn_w1, L0_ffn_conv_w, L0_ffn_conv_b, L0_ffn_w2, L0_c, L0_wmod):
    f32 = np.float32
    com = {
        "wk": np.ascontiguousarray(L0_attn_Wk, dtype=f32),
        "wq": np.ascontiguousarray(L0_attn_Wq, dtype=f32),
        "wv": np.ascontiguousarray(L0_attn_Wv, dtype=f32),
        "wb": np.ascontiguousarray(L0_attn_Wbeta, dtype=f32),
        "wo": np.ascontiguousarray(L0_attn_Wout, dtype=f32),
        "w1t": np.ascontiguousarray(np.asarray(L0_ffn_w1, dtype=f32).T),
        "w2t": np.ascontiguousarray(np.asarray(L0_ffn_w2, dtype=f32).T),
        "cw": np.ascontiguousarray(
            np.asarray(L0_ffn_conv_w, dtype=f32).reshape(NCH, 128, 3)
            .transpose(1, 0, 2).reshape(128, 3 * NCH)),
        "cb": np.ascontiguousarray(
            np.asarray(L0_ffn_conv_b, dtype=f32).reshape(NCH, 128).T),
        "wmod": np.ascontiguousarray(L0_wmod, dtype=f32),
    }
    maps = []
    for i in range(8):
        m = dict(com)
        m["x"] = np.ascontiguousarray(
            np.asarray(L0_attn_x[2 * i:2 * i + 2], dtype=f32).transpose(0, 2, 1))
        m["cvec"] = np.ascontiguousarray(L0_c[2 * i:2 * i + 2], dtype=f32)
        maps.append(m)
    return maps


def kernel(**inputs):
    maps = _shards(**inputs)
    nc = build_program(DBG if 'DBG' in globals() else ())
    res = run_bass_kernel_spmd(nc, maps, core_ids=list(range(8)))
    out = np.concatenate([r["out"].transpose(0, 2, 1) for r in res.results], axis=0)
    return np.ascontiguousarray(out, dtype=np.float32)


DBG = ()
